# version 02
# baseline (speedup 1.0000x reference)
import numpy as np
import ml_dtypes
import concourse.bass as bass
import concourse.mybir as mybir
from concourse.bass_utils import run_bass_kernel_spmd
from contextlib import ExitStack

F32 = mybir.dt.float32
BF16 = mybir.dt.bfloat16
I32 = mybir.dt.int32
AF = mybir.ActivationFunctionType
ALU = mybir.AluOpType
BF = ml_dtypes.bfloat16

B, S, D = 2, 8192, 4096
NH, NKV, HD = 16, 4, 64
LATENT, KVD = 1024, 256
NCORE = 8
TPC = 2048
HALO = 4
STRIP = HALO + TPC
NE, RANK = 4, 8
EPS = 1e-5

HALF = 1028
SUBS = [(0, 4), (4, 516), (516, 1028)]
REAL_SUBS = [(4, 516), (516, 1028)]



def _ul_q_orig(o, j):
    if o < 4:
        h = 4 * o + j // 32
        d = 2 * (j % 32)
    else:
        h = 4 * (o - 4) + j // 32
        d = 2 * (j % 32) + 1
    return h * 64 + d


def _ul_k_orig(o, j):
    g = j // 32
    d = 2 * (j % 32) + (1 if o == 1 else 0)
    return g * 64 + d


def _prep_shared(ins):
    f = np.float32
    sh = {}
    W = np.asarray(ins["L0_ffn_W_up"], f)
    sh["wup_t"] = np.ascontiguousarray(
        W.reshape(32, 128, 32, 128).transpose(0, 2, 3, 1))
    Wq = np.asarray(ins["L0_attn_w_qkv"], f)
    sh["wqkv_t"] = np.ascontiguousarray(
        Wq.reshape(12, 128, 32, 128).transpose(0, 2, 3, 1))
    Wo = np.asarray(ins["L0_attn_w_o"], f)
    sh["wo_t"] = np.ascontiguousarray(
        Wo.reshape(32, 128, 8, 128).transpose(0, 2, 3, 1))

    A = np.asarray(ins["L0_ffn_lora_A"], f)
    G = np.asarray(ins["L0_ffn_gate_w"], f)
    ag = np.zeros((32, 128, 36), f)
    for e in range(NE):
        for r in range(RANK):
            ag[:, :, 8 * e + r] = A[e, r].reshape(32, 128)
    for e in range(NE):
        ag[:, :, 32 + e] = G[e].reshape(32, 128)
    sh["ag36"] = ag
    Bl = np.asarray(ins["L0_ffn_lora_B"], f)
    lb = np.zeros((32, 32, 128), f)
    for e in range(NE):
        for r in range(RANK):
            lb[:, 8 * e + r, :] = Bl[e, :, r].reshape(32, 128)
    sh["lorab_t"] = lb
    sh["eb4"] = np.asarray(ins["L0_ffn_expert_bias"], f).reshape(4, 1)

    qc1 = np.asarray(ins["L0_attn_qc1_w"], f)
    q1 = np.zeros((8, 3, 128, 128), f)
    for c in range(8):
        for tap in range(3):
            for jj in range(128):
                och = 128 * c + jj
                base = 64 * (jj // 64)
                q1[c, tap, base:base + 64, jj] = qc1[och, :, tap]
    sh["qc1_t"] = q1
    kc1 = np.asarray(ins["L0_attn_kc1_w"], f)
    k1 = np.zeros((2, 3, 128, 128), f)
    for c in range(2):
        for tap in range(3):
            for jj in range(128):
                och = 128 * c + jj
                base = 64 * (jj // 64)
                k1[c, tap, base:base + 64, jj] = kc1[och, :, tap]
    sh["kc1_t"] = k1

    qc2 = np.asarray(ins["L0_attn_qc2_w"], f)
    q2 = np.zeros((8, 8, 3, 128, 128), f)
    ul_rows_q = np.array([[_ul_q_orig(o, j) for j in range(128)]
                          for o in range(8)])
    for o in range(8):
        block = qc2[ul_rows_q[o]]
        q2[o] = block.reshape(128, 8, 128, 3).transpose(1, 3, 2, 0)
    sh["qc2_t"] = np.ascontiguousarray(q2)
    kc2 = np.asarray(ins["L0_attn_kc2_w"], f)
    k2 = np.zeros((2, 2, 3, 128, 128), f)
    ul_rows_k = np.array([[_ul_k_orig(o, j) for j in range(128)]
                          for o in range(2)])
    for o in range(2):
        block = kc2[ul_rows_k[o]]
        k2[o] = block.reshape(128, 2, 128, 3).transpose(1, 3, 2, 0)
    sh["kc2_t"] = np.ascontiguousarray(k2)

    sh["qc1b"] = np.ascontiguousarray(
        np.asarray(ins["L0_attn_qc1_b"], f).reshape(8, 128).T)
    sh["kc1b"] = np.ascontiguousarray(
        np.asarray(ins["L0_attn_kc1_b"], f).reshape(2, 128).T)
    qb2 = np.asarray(ins["L0_attn_qc2_b"], f)
    sh["qc2b"] = np.ascontiguousarray(
        np.stack([qb2[ul_rows_q[o]] for o in range(8)]).T)
    kb2 = np.asarray(ins["L0_attn_kc2_b"], f)
    sh["kc2b"] = np.ascontiguousarray(
        np.stack([kb2[ul_rows_k[o]] for o in range(2)]).T)

    comb = np.zeros((128, 64), f)
    for slot in range(128):
        comb[slot, slot % 64] = 1.0
    sh["combm"] = comb
    eq = np.zeros((2, 64, 128), f)
    ek = np.zeros((2, 64, 128), f)
    for lu in range(2):
        for j in range(128):
            d = 2 * (j % 32) + lu
            eq[lu, d, j] = 0.5 / 4.0
            ek[lu, d, j] = 0.5 / 16.0
    sh["eq_ul"] = eq
    sh["ek_ul"] = ek

    l2q = np.zeros((8, 128, 16), f)
    for u in range(8):
        for slot in range(128):
            l2q[u, slot, 4 * (u % 4) + slot // 32] = 1.0
    sh["l2q"] = l2q
    l2k = np.zeros((128, 4), f)
    for slot in range(128):
        l2k[slot, slot // 32] = 1.0
    sh["l2k"] = l2k
    erq = np.zeros((4, 16, 128), f)
    for u in range(4):
        for j in range(128):
            erq[u, 4 * u + j // 32, j] = 1.0
    sh["erq"] = erq
    erk = np.zeros((4, 128), f)
    for j in range(128):
        erk[j // 32, j] = 1.0
    sh["erk"] = erk

    ident = np.eye(128, dtype=f)
    sh["ident"] = ident
    sh["ones_col"] = np.ones((128, 1), f)
    sh["ones_row"] = np.ones((1, 128), f)
    e432 = np.zeros((4, 32), f)
    for e in range(NE):
        e432[e, 8 * e:8 * e + 8] = 1.0
    sh["exp4to32"] = e432
    sh["kt_rep"] = np.repeat(np.asarray(ins["L0_attn_key_temp"], f).reshape(1, 1),
                             128, axis=0)
    return sh


def _prep_core_l1(ins, sh, c):
    f = np.float32
    s, blk = c // 4, c % 4
    t0 = TPC * blk
    x = np.asarray(ins["L0_attn_x"], f)[s]
    xs = np.zeros((STRIP, D), f)
    lo = max(0, t0 - HALO)
    xs[HALO - (t0 - lo):] = x[lo:t0 + TPC]
    x_fm = np.ascontiguousarray(
        xs.T.reshape(32, 128, STRIP))
    pos = np.arange(t0, t0 + TPC, dtype=np.float64)
    invf = 10000.0 ** (-np.arange(0, HD, 2, dtype=np.float64) / HD)
    slot_f = invf[np.arange(128) % 32]
    ang = slot_f[:, None] * pos[None, :]
    cos_t = np.cos(ang).astype(f)
    sin_t = np.sin(ang).astype(f)
    d = {"x_fm": x_fm, "cos_t": cos_t, "sin_t": sin_t,
         "halo01": np.full((128, 1), 0.0 if t0 == 0 else 1.0, f)}
    for k in ("wup_t", "wqkv_t", "ag36", "lorab_t", "eb4", "qc1_t", "kc1_t",
              "qc2_t", "kc2_t", "qc1b", "kc1b", "qc2b", "kc2b", "combm",
              "eq_ul", "ek_ul", "l2q", "l2k", "erq", "erk", "ident",
              "ones_row", "exp4to32", "kt_rep"):
        d[k] = sh[k]
    return d



class SemBook:

    def __init__(self, nc, es, names):
        self.h = {}
        self.v = {}
        for n in names:
            self.h[n] = es.enter_context(nc.semaphore(n))
            self.v[n] = 0

    def bump(self, n, amt):
        self.v[n] += amt
        return self.h[n], self.v[n]


def _sb(nc, es, name, shape, dtype):
    return es.enter_context(nc.sbuf_tensor(name, shape, dtype))


def _ps(nc, es, name, shape, dtype=F32):
    return es.enter_context(nc.psum_tensor(name, shape, dtype))


def _rear(ap, dims, offset=0):
    return bass.AP(ap.tensor, offset, [list(d) for d in dims])


_DTSZ = {F32: 4, BF16: 2, I32: 4}


class Arena:

    def __init__(self, nc, es, nbytes):
        self.t = es.enter_context(nc.sbuf_tensor("ARENA", [128, nbytes],
                                                 mybir.dt.uint8))
        self.off = 0
        self.nbytes = nbytes

    def reset(self):
        self.off = 0

    def take(self, rows, cols, dtype):
        sz = cols * _DTSZ[dtype]
        self.off = (self.off + 63) & ~63
        assert self.off + sz <= self.nbytes, (self.off, sz, self.nbytes)
        v = self.t[0:rows, self.off:self.off + sz].bitcast(dtype)
        self.off += sz
        return v





class Op:
    __slots__ = ("eng", "fn", "deps", "is_dma", "seq", "need_inc", "semval",
                 "tag", "pool_i", "pool_use")

    def __init__(self, eng, fn, is_dma, tag=""):
        self.eng = eng
        self.fn = fn
        self.deps = []
        self.is_dma = is_dma
        self.seq = 0
        self.need_inc = False
        self.semval = 0
        self.tag = tag


class Prog:

    ENGS = ("t", "v", "s", "g", "y")

    def __init__(self, nc):
        self.nc = nc
        self.ops = {e: [] for e in self.ENGS}
        self.last_writer = {}
        self.readers = {}
        self.all = []
        self.barrier_deps = []

    def barrier(self):
        self.barrier_deps = [self.ops[e][-1] for e in self.ENGS if self.ops[e]]

    def _engine(self, e):
        return {"t": self.nc.tensor, "v": self.nc.vector, "s": self.nc.scalar,
                "g": self.nc.gpsimd, "y": self.nc.sync}[e]

    def op(self, eng, fn, reads=(), writes=(), is_dma=False, tag=""):
        o = Op(eng, fn, is_dma, tag)
        o.seq = len(self.ops[eng])
        deps = list(self.barrier_deps)
        for r in reads:
            w = self.last_writer.get(r)
            if w is not None:
                deps.append(w)
        for r in writes:
            w = self.last_writer.get(r)
            if w is not None:
                deps.append(w)
            deps.extend(self.readers.get(r, ()))
        seen = set()
        for d in deps:
            if d is o or id(d) in seen:
                continue
            seen.add(id(d))
            if d.eng == eng and eng == "t" and not d.is_dma:
                continue
            o.deps.append(d)
        for r in reads:
            self.readers.setdefault(r, []).append(o)
        for r in writes:
            self.last_writer[r] = o
            self.readers[r] = []
        self.ops[eng].append(o)
        self.all.append(o)
        return o

    NPOOL = 6

    def emit(self, es):
        for o in self.all:
            if o.is_dma:
                o.need_inc = True
            for d in o.deps:
                d.need_inc = True
        sems = {}
        pool = {}
        for e in self.ENGS:
            sems[e] = es.enter_context(self.nc.semaphore(f"prog_{e}"))
            pool[e] = [es.enter_context(self.nc.semaphore(f"dma_{e}{i}"))
                       for i in range(self.NPOOL)]
            cnt = 0
            ndma = 0
            uses = [0] * self.NPOOL
            for o in self.ops[e]:
                if o.is_dma:
                    o.pool_i = ndma % self.NPOOL
                    uses[o.pool_i] += 1
                    o.pool_use = uses[o.pool_i]
                    o.semval = 16 * o.pool_use
                    ndma += 1
                else:
                    if o.need_inc:
                        cnt += 1
                    o.semval = cnt
        for e in self.ENGS:
            eng = self._engine(e)
            seen_wm = {}
            for o in self.ops[e]:
                need = {}
                for d in o.deps:
                    if d.is_dma:
                        key = (d.eng, "p", d.pool_i)
                    else:
                        key = (d.eng, "c")
                        if d.semval == 0:
                            continue
                    need[key] = max(need.get(key, 0), d.semval)
                if o.is_dma and o.pool_use > 1:
                    key = (e, "p", o.pool_i)
                    need[key] = max(need.get(key, 0), 16 * (o.pool_use - 1))
                for key, v in need.items():
                    if v > seen_wm.get(key, 0):
                        f = key[0]
                        h = sems[f] if key[1] == "c" else pool[f][key[2]]
                        eng.wait_ge(h, v)
                        seen_wm[key] = v
                ins = o.fn(eng)
                if o.is_dma:
                    ins.then_inc(pool[e][o.pool_i], 16)
                elif o.need_inc:
                    assert ins is not None, f"op {o.tag} needs inc, fn gave None"
                    ins.then_inc(sems[e], 1)
        return sems, pool



L1_DEBUG = False


def build_launch1():
    nc = bass.Bass("TRN2", target_bir_lowering=False)
    es = ExitStack()
    p = Prog(nc)

    def din(name, shape, dt=F32):
        return nc.dram_tensor(name, shape, dt, kind="ExternalInput").ap()

    x_fm = din("x_fm", [32, 128, STRIP])
    wup_t = din("wup_t", [32, 32, 128, 128])
    wqkv_t = din("wqkv_t", [12, 32, 128, 128])
    ag36 = din("ag36", [32, 128, 36])
    lorab_t = din("lorab_t", [32, 32, 128])
    eb4 = din("eb4", [4, 1])
    qc1_t = din("qc1_t", [8, 3, 128, 128])
    kc1_t = din("kc1_t", [2, 3, 128, 128])
    qc2_t = din("qc2_t", [8, 8, 3, 128, 128])
    kc2_t = din("kc2_t", [2, 2, 3, 128, 128])
    qc1b = din("qc1b", [128, 8])
    kc1b = din("kc1b", [128, 2])
    qc2b = din("qc2b", [128, 8])
    kc2b = din("kc2b", [128, 2])
    combm = din("combm", [128, 64])
    eq_ul = din("eq_ul", [2, 64, 128])
    ek_ul = din("ek_ul", [2, 64, 128])
    l2q = din("l2q", [8, 128, 16])
    l2k = din("l2k", [128, 4])
    erq = din("erq", [4, 16, 128])
    erk = din("erk", [4, 128])
    ident = din("ident", [128, 128])
    ones_row = din("ones_row", [1, 128])
    exp4to32 = din("exp4to32", [4, 32])
    cos_in = din("cos_t", [128, TPC])
    sin_in = din("sin_t", [128, TPC])
    kt_rep = din("kt_rep", [128, 1])
    halo01 = din("halo01", [128, 1])

    x1_out = nc.dram_tensor("x1_out", [32, 128, TPC], F32, kind="ExternalOutput").ap()
    qblob = nc.dram_tensor("qblob", [8, 128, TPC], BF16, kind="ExternalOutput").ap()
    kblob = nc.dram_tensor("kblob", [2, 128, TPC], BF16, kind="ExternalOutput").ap()
    vblob = nc.dram_tensor("vblob", [TPC, 256], BF16, kind="ExternalOutput").ap()
    qkv_sp = nc.dram_tensor("qkv_sp", [12, 128, STRIP], BF16, kind="Internal").ap()
    if L1_DEBUG:
        dbg_mu = nc.dram_tensor("dbg_mu", [1, HALF], BF16, kind="ExternalOutput").ap()
        dbg_r = nc.dram_tensor("dbg_r", [1, HALF], BF16, kind="ExternalOutput").ap()
        dbg_lnx = nc.dram_tensor("dbg_lnx", [32, 128, HALF], BF16, kind="ExternalOutput").ap()
        dbg_zm = nc.dram_tensor("dbg_zm", [32, HALF], BF16, kind="ExternalOutput").ap()
        dbg_lsb = nc.dram_tensor("dbg_lsb", [4, HALF], F32, kind="ExternalOutput").ap()
        dbg_tm = nc.dram_tensor("dbg_tm", [128, 36], F32, kind="ExternalOutput").ap()
        dbg_oh = nc.dram_tensor("dbg_oh", [128, 36], F32, kind="ExternalOutput").ap()
        dbg_ohfm = nc.dram_tensor("dbg_ohfm", [4, 1040], BF16, kind="ExternalOutput").ap()

    eb4sb = _sb(nc, es, "eb4sb", [4, 1], F32)
    onesrbf = _sb(nc, es, "onesrbf", [1, 128], BF16)
    identbf = _sb(nc, es, "identbf", [128, 128], BF16)
    e432bf = _sb(nc, es, "e432bf", [4, 32], BF16)
    halosb = _sb(nc, es, "halosb", [128, 1], F32)

    P0 = _ps(nc, es, "P0", [128, 512])
    P1 = _ps(nc, es, "P1", [128, 512])
    P2 = _ps(nc, es, "B2", [128, 512])
    P3 = _ps(nc, es, "B3", [128, 512])
    P4 = _ps(nc, es, "B4", [128, 512])
    P5 = _ps(nc, es, "B5", [128, 512])
    P6 = _ps(nc, es, "B6", [64, 512])
    P7 = _ps(nc, es, "B7", [128, 128], BF16)
    PM = [P0, P1]

    p.op("g", lambda g: g.dma_start(out=ag36bf[:, :],
         in_=_rear(ag36, [(36, 128), (4608, 32), (1, 36)])),
         writes=["ag36bf"], is_dma=True)
    p.op("g", lambda g: g.dma_start(out=lorabbf[:, :],
         in_=_rear(lorab_t, [(128, 32), (4096, 32), (1, 128)])),
         writes=["lorabbf"], is_dma=True)
    p.op("g", lambda g: g.dma_start(out=onesrbf[:, :], in_=ones_row),
         writes=["onesrbf"], is_dma=True)
    p.op("g", lambda g: g.dma_start(out=identbf[:, :], in_=ident),
         writes=["identbf"], is_dma=True)
    p.op("g", lambda g: g.dma_start(out=e432bf[:, :], in_=exp4to32),
         writes=["e432bf"], is_dma=True)
    p.op("y", lambda y: y.dma_start(out=eb4sb[:, :], in_=eb4),
         writes=["eb4sb"], is_dma=True)
    p.op("y", lambda y: y.dma_start(out=halosb[:, :], in_=halo01),
         writes=["halosb"], is_dma=True)

    ar = Arena(nc, es, 206000)
    AR1 = ar.take(128, 32 * HALF, BF16)
    AR2 = ar.take(128, 32 * HALF, BF16)
    ag36bf = ar.take(128, 32 * 36, BF16)
    lorabbf = ar.take(32, 32 * 128, BF16)
    wt = [ar.take(128, 4096, BF16) for i in range(2)]
    xsq = [ar.take(128, HALF, BF16) for i in range(2)]
    tmpc = xsq[0]
    mu128 = ar.take(128, HALF, BF16)
    r128 = ar.take(128, HALF, BF16)
    rowA = ar.take(1, HALF, F32)
    rowB = ar.take(1, HALF, F32)
    rowC = None
    mubf = ar.take(1, HALF, BF16)
    rbf = ar.take(1, HALF, BF16)
    zsb = ar.take(32, HALF, F32)
    zm = ar.take(32, HALF, BF16)
    lsb = ar.take(4, HALF, F32)
    rowC = lsb[0:1, 0:HALF]
    lsbbf = ar.take(4, HALF, BF16)
    tmsb = ar.take(128, 36, F32)
    mxsb = ar.take(128, 9, F32)
    gesb = ar.take(128, 36, F32)
    ohsb = ar.take(128, 36, F32)
    notsb = ar.take(128, 9, F32)
    ohbf = ar.take(128, 36, BF16)
    ohfm = ar.take(4, 1040, BF16)
    x1f = [ar.take(128, 512, F32) for i in range(2)]
    qev = [xsq[i][:, 0:512] for i in range(2)]
    xres = [ar.take(128, 512, F32) for i in range(2)]

    def A1(k):
        return AR1[:, HALF * k:HALF * (k + 1)]

    def A2(k):
        return AR2[:, HALF * k:HALF * (k + 1)]

    def stats_rows_pass(src_of_k, key_of_k):
        PSX = [P0, P1, P2]
        PSQ = [P3, P4, P5]
        for k in range(32):
            par = k % 2
            p.op("s", lambda s, k=k, par=par: s.activation(
                out=xsq[par][:, :], in_=src_of_k(k), func=AF.Square),
                reads=[key_of_k(k)], writes=[f"xsq{par}"])
            for si, (a, b) in enumerate(SUBS):
                w = b - a
                p.op("t", lambda t, k=k, a=a, w=w, si=si: t.matmul(
                    PSX[si][0:1, 0:w], onescbf_v[:, :], src_of_k(k)[:, a:a + w],
                    start=(k == 0), stop=(k == 31)),
                    reads=[key_of_k(k), "onescbf"], writes=[("B0","B1","B2")[si]])
                p.op("t", lambda t, k=k, a=a, w=w, par=par, si=si: t.matmul(
                    PSQ[si][0:1, 0:w], onescbf_v[:, :], xsq[par][:, a:a + w],
                    start=(k == 0), stop=(k == 31)),
                    reads=[f"xsq{par}", "onescbf"], writes=[("B3","B4","B5")[si]])
        def ev_rows(v):
            v.tensor_copy(rowA[0:1, 0:4], PSX[0][0:1, 0:4])
            v.tensor_copy(rowA[0:1, 4:516], PSX[1][0:1, 0:512])
            v.tensor_copy(rowA[0:1, 516:1028], PSX[2][0:1, 0:512])
            v.tensor_copy(rowB[0:1, 0:4], PSQ[0][0:1, 0:4])
            v.tensor_copy(rowB[0:1, 4:516], PSQ[1][0:1, 0:512])
            return v.tensor_copy(rowB[0:1, 516:1028], PSQ[2][0:1, 0:512])
        p.op("v", ev_rows,
             reads=["B0", "B1", "B2", "B3", "B4", "B5"],
             writes=["rowA", "rowB"])
        p.op("v", lambda v: v.tensor_scalar(
            out=rowA[:, :], in0=rowA[:, :], scalar1=1.0 / 4096,
            scalar2=None, op0=ALU.mult), reads=["rowA"], writes=["rowA"])
        p.op("v", lambda v: v.tensor_scalar(
            out=rowB[:, :], in0=rowB[:, :], scalar1=1.0 / 4096,
            scalar2=None, op0=ALU.mult), reads=["rowB"], writes=["rowB"])
        p.op("v", lambda v: v.tensor_tensor(
            out=rowC[:, :], in0=rowA[:, :], in1=rowA[:, :], op=ALU.mult),
            reads=["rowA"], writes=["lsb"])
        p.op("v", lambda v: v.tensor_tensor(
            out=rowB[:, :], in0=rowB[:, :], in1=rowC[:, :], op=ALU.subtract),
            reads=["rowB", "lsb"], writes=["rowB"])
        p.op("v", lambda v: v.tensor_scalar(
            out=rowB[:, :], in0=rowB[:, :], scalar1=EPS, scalar2=None,
            op0=ALU.add), reads=["rowB"], writes=["rowB"])
        p.op("s", lambda s: s.activation(out=rowC[:, :], in_=rowB[:, :],
                                         func=AF.Ln),
             reads=["rowB"], writes=["lsb"])
        p.op("s", lambda s: s.activation(out=rowC[:, :], in_=rowC[:, :],
                                         func=AF.Exp, scale=-0.5),
             reads=["lsb"], writes=["lsb"])
        p.op("v", lambda v: v.tensor_copy(mubf[:, :], rowA[:, :]),
             reads=["rowA"], writes=["mubf"])
        p.op("v", lambda v: v.tensor_copy(rbf[:, :], rowC[:, :]),
             reads=["lsb"], writes=["rbf"])
        for si, (a, b) in enumerate(SUBS):
            w = b - a
            p.op("t", lambda t, a=a, w=w: t.matmul(
                P4[:, 0:w], onesrbf[0:1, :], mubf[0:1, a:a + w],
                start=True, stop=True),
                reads=["mubf", "onesrbf"], writes=["B4"])
            p.op("t", lambda t, a=a, w=w: t.matmul(
                P5[:, 0:w], onesrbf[0:1, :], rbf[0:1, a:a + w],
                start=True, stop=True),
                reads=["rbf", "onesrbf"], writes=["B5"])
            p.op("v", lambda v, a=a, w=w: v.tensor_copy(
                mu128[:, a:a + w], P4[:, 0:w]), reads=["B4"], writes=["mu128"])
            p.op("v", lambda v, a=a, w=w: v.tensor_copy(
                r128[:, a:a + w], P5[:, 0:w]), reads=["B5"], writes=["r128"])

    def ln_build(src_of_k, skey, dst_of_k, dkey):
        for k in range(32):
            p.op("v", lambda v, k=k: v.tensor_sub(
                tmpc[:, :], src_of_k(k), mu128[:, :]),
                reads=[skey(k), "mu128"], writes=["xsq0"])
            p.op("v", lambda v, k=k: v.tensor_mul(
                dst_of_k(k), tmpc[:, :], r128[:, :]),
                reads=["xsq0", "r128"], writes=[dkey(k)])

    def gate_pass(lnx_of_k, lnkey):
        for k in range(32):
            for si, (a, b) in enumerate(SUBS):
                dst = [P6, P4, P5][si]
                dkey = ["B6", "B4", "B5"][si]
                w = b - a
                p.op("t", lambda t, k=k, a=a, w=w, dst=dst: t.matmul(
                    dst[0:36, 0:w], ag36bf[:, 36 * k:36 * (k + 1)],
                    lnx_of_k(k)[:, a:a + w], start=(k == 0), stop=(k == 31)),
                    reads=[lnkey(k), "ag36bf"], writes=[dkey])
        def ev_z(v):
            v.tensor_copy(zsb[:, 4:516], P4[0:32, 0:512])
            v.tensor_copy(zsb[:, 516:1028], P5[0:32, 0:512])
            v.tensor_copy(zsb[:, 0:4], P6[0:32, 0:4])
            v.tensor_copy(lsb[:, 4:516], P4[32:36, 0:512])
            v.tensor_copy(lsb[:, 516:1028], P5[32:36, 0:512])
            return v.tensor_copy(lsb[:, 0:4], P6[32:36, 0:4])
        p.op("v", ev_z, reads=["B4", "B5", "B6"], writes=["zsb", "lsb"])
        p.op("v", lambda v: v.tensor_scalar(
            out=lsb[:, :], in0=lsb[:, :], scalar1=eb4sb[:, :],
            scalar2=None, op0=ALU.add), reads=["lsb", "eb4sb"], writes=["lsb"])
        p.op("v", lambda v: v.tensor_copy(lsbbf[:, :], lsb[:, :]),
             reads=["lsb"], writes=["lsbbf"])
        for t8 in range(9):
            w = 128 if t8 < 8 else 4
            p.op("t", lambda t, t8=t8, w=w: t.transpose(
                P7[0:w, 4 * t8:4 * t8 + 4], lsbbf[0:4, 128 * t8:128 * t8 + w],
                identbf[0:4, 0:4]),
                reads=["lsbbf", "identbf"], writes=["B7"])
        def ev_tm(v):
            tm3 = P7[:, 0:36].rearrange("p (t f) -> p t f", f=4)
            for e in range(4):
                v.tensor_copy(tmsb[:, 9 * e:9 * e + 9], tm3[:, :, e])
            return v.tensor_copy(mxsb[:, :], tm3[:, :, 0])
        p.op("v", ev_tm, reads=["B7"], writes=["tm", "mx"])
        p.op("v", lambda v: v.tensor_tensor(
            out=notsb[:, :], in0=tmsb[:, 0:9], in1=tmsb[:, 9:18], op=ALU.max),
            reads=["tm"], writes=["not"])
        p.op("v", lambda v: v.tensor_tensor(
            out=mxsb[:, :], in0=tmsb[:, 18:27], in1=tmsb[:, 27:36], op=ALU.max),
            reads=["tm"], writes=["mx"])
        p.op("v", lambda v: v.tensor_tensor(
            out=mxsb[:, :], in0=mxsb[:, :], in1=notsb[:, :], op=ALU.max),
            reads=["mx", "not"], writes=["mx"])
        for e in range(4):
            p.op("v", lambda v, e=e: v.tensor_tensor(
                out=gesb[:, 9 * e:9 * e + 9], in0=tmsb[:, 9 * e:9 * e + 9],
                in1=mxsb[:, :], op=ALU.is_ge),
                reads=["tm", "mx"], writes=["ge"])
        p.op("v", lambda v: v.tensor_copy(ohsb[:, 0:9], gesb[:, 0:9]),
             reads=["ge"], writes=["oh"])
        p.op("v", lambda v: v.tensor_scalar(
            out=notsb[:, :], in0=gesb[:, 0:9], scalar1=-1.0, scalar2=1.0,
            op0=ALU.mult, op1=ALU.add), reads=["ge"], writes=["not"])
        for e in range(1, 4):
            p.op("v", lambda v, e=e: v.tensor_tensor(
                out=ohsb[:, 9 * e:9 * e + 9], in0=gesb[:, 9 * e:9 * e + 9],
                in1=notsb[:, :], op=ALU.mult),
                reads=["ge", "not"], writes=["oh"])
            if e < 3:
                p.op("v", lambda v, e=e: v.tensor_scalar(
                    out=mxsb[:, :], in0=gesb[:, 9 * e:9 * e + 9],
                    scalar1=-1.0, scalar2=1.0, op0=ALU.mult, op1=ALU.add),
                    reads=["ge"], writes=["mx"])
                p.op("v", lambda v: v.tensor_tensor(
                    out=notsb[:, :], in0=notsb[:, :], in1=mxsb[:, :],
                    op=ALU.mult), reads=["not", "mx"], writes=["not"])
        p.op("v", lambda v: v.tensor_copy(ohbf[:, :], ohsb[:, :]),
             reads=["oh"], writes=["ohbf"])
        for t8 in range(9):
            w = 128 if t8 < 8 else 4
            par = t8 % 2
            rowb = 0 if par == 0 else 64
            p.op("t", lambda t, t8=t8, w=w, rowb=rowb: t.transpose(
                P7[rowb:rowb + 4, 0:w],
                ohbf[0:w, :].rearrange("p (e t) -> p t e", t=9)[:, t8, :],
                identbf[0:w, 0:w]),
                reads=["ohbf", "identbf"], writes=["B7", f"B7x{par}"])
            p.op("v", lambda v, t8=t8, w=w, rowb=rowb: v.tensor_copy(
                ohfm[0:4, 128 * t8:128 * t8 + w], P7[rowb:rowb + 4, 0:w]),
                reads=[f"B7x{par}"], writes=["ohfm"])
        for si, (a, b) in enumerate(SUBS):
            dst = [P6, P4, P5][si]
            dkey = ["B6", "B4", "B5"][si]
            w = b - a
            p.op("t", lambda t, a=a, w=w, dst=dst: t.matmul(
                dst[0:32, 0:w], e432bf[:, :], ohfm[0:4, a:a + w],
                start=True, stop=True),
                reads=["ohfm", "e432bf"], writes=[dkey])
            p.op("v", lambda v, a=a, w=w, dst=dst: v.tensor_tensor(
                out=zm[:, a:a + w], in0=zsb[:, a:a + w], in1=dst[0:32, 0:w],
                op=ALU.mult),
                reads=[dkey, "zsb"], writes=["zm"])

    onescbf_v = _sb(nc, es, "onescbf", [128, 1], BF16)
    p.op("v", lambda v: v.memset(onescbf_v[:, :], 1.0), writes=["onescbf"])

    for h in range(2):
        base = 1024 * h
        for k in range(32):
            p.op("g", lambda g, k=k, base=base: g.dma_start(
                out=A1(k), in_=x_fm[k][:, base:base + HALF]),
                writes=[f"AR1:{k}"], is_dma=True, tag=f"xld{h}:{k}")
        stats_rows_pass(A1, lambda k: f"AR1:{k}")
        ln_build(A1, lambda k: f"AR1:{k}", A2, lambda k: f"AR2:{k}")
        gate_pass(A2, lambda k: f"AR2:{k}")
        if L1_DEBUG and h == 0:
            p.op("y", lambda y: y.dma_start(out=dbg_lsb, in_=lsb[:, :]),
                 reads=["lsb"], writes=["OUT"], is_dma=True)
            p.op("y", lambda y: y.dma_start(out=dbg_tm, in_=tmsb[:, :]),
                 reads=["tm"], writes=["OUT"], is_dma=True)
            p.op("y", lambda y: y.dma_start(out=dbg_oh, in_=ohsb[:, :]),
                 reads=["oh"], writes=["OUT"], is_dma=True)
            p.op("y", lambda y: y.dma_start(out=dbg_ohfm, in_=ohfm[:, :]),
                 reads=["ohfm"], writes=["OUT"], is_dma=True)
            p.op("y", lambda y: y.dma_start(out=dbg_mu, in_=mubf[:, :]),
                 reads=["mubf"], writes=["OUT"], is_dma=True)
            p.op("y", lambda y: y.dma_start(out=dbg_r, in_=rbf[:, :]),
                 reads=["rbf"], writes=["OUT"], is_dma=True)
            for k in range(32):
                p.op("y", lambda y, k=k: y.dma_start(out=dbg_lnx[k], in_=A2(k)),
                     reads=[f"AR2:{k}"], writes=["OUT"], is_dma=True)
            p.op("y", lambda y: y.dma_start(out=dbg_zm, in_=zm[:, :]),
                 reads=["zm"], writes=["OUT"], is_dma=True)
        for o in range(32):
            wpar = o % 2
            p.op("g", lambda g, o=o, wpar=wpar: g.dma_start(
                out=wt[wpar][:, :],
                in_=_rear(wup_t, [(128, 128), (16384, 32), (1, 128)],
                          offset=o * 32 * 128 * 128)),
                writes=[f"wt{wpar}"], is_dma=True, tag=f"wup{h}:{o}")
            for si, (a, b) in enumerate(SUBS):
                gi = 3 * o + si
                par = gi % 2
                w = b - a
                for k in range(32):
                    p.op("t", lambda t, k=k, a=a, w=w, par=par, wpar=wpar: t.matmul(
                        PM[par][:, 0:w], wt[wpar][:, 128 * k:128 * (k + 1)],
                        A2(k)[:, a:a + w], start=(k == 0), stop=False),
                        reads=[f"wt{wpar}", f"AR2:{k}"], writes=[f"B{par}"])
                p.op("t", lambda t, a=a, w=w, par=par, o=o: t.matmul(
                    PM[par][:, 0:w], lorabbf[:, 128 * o:128 * (o + 1)],
                    zm[:, a:a + w], start=False, stop=True),
                    reads=["lorabbf", "zm"], writes=[f"B{par}"])
                p.op("y", lambda y, o=o, a=a, w=w, par=par, base=base: y.dma_start(
                    out=xres[par][:, 0:w], in_=x_fm[o][:, base + a:base + a + w]),
                    writes=[f"xres{par}"], is_dma=True)
                p.op("v", lambda v, o=o, a=a, w=w, par=par: v.scalar_tensor_tensor(
                    out=x1f[par][:, 0:w], in0=PM[par][:, 0:w], scalar=0.5,
                    in1=xres[par][:, 0:w], op0=ALU.mult, op1=ALU.add),
                    reads=[f"B{par}", f"xres{par}"], writes=[f"x1f{par}"])
                p.op("v", lambda v, o=o, a=a, w=w, par=par: v.tensor_copy(
                    A1(o)[:, a:a + w], x1f[par][:, 0:w]),
                    reads=[f"x1f{par}"], writes=[f"AR1:{o}"])
                if si > 0:
                    p.op("y", lambda y, o=o, a=a, w=w, par=par, base=base: y.dma_start(
                        out=x1_out[o][:, base + a - HALO:base + a - HALO + w],
                        in_=x1f[par][:, 0:w]),
                        reads=[f"x1f{par}"], writes=["OUT"], is_dma=True)
        stats_rows_pass(A1, lambda k: f"AR1:{k}")
        ln_build(A1, lambda k: f"AR1:{k}", A2, lambda k: f"AR2:{k}")
        for o in range(12):
            wpar = o % 2
            p.op("g", lambda g, o=o, wpar=wpar: g.dma_start(
                out=wt[wpar][:, :],
                in_=_rear(wqkv_t, [(128, 128), (16384, 32), (1, 128)],
                          offset=o * 32 * 128 * 128)),
                writes=[f"wt{wpar}"], is_dma=True, tag=f"wqkv{h}:{o}")
            for si, (a, b) in enumerate(SUBS):
                gi = 3 * o + si
                par = gi % 2
                w = b - a
                for k in range(32):
                    p.op("t", lambda t, k=k, a=a, w=w, par=par, wpar=wpar: t.matmul(
                        PM[par][:, 0:w], wt[wpar][:, 128 * k:128 * (k + 1)],
                        A2(k)[:, a:a + w], start=(k == 0), stop=(k == 31)),
                        reads=[f"wt{wpar}", f"AR2:{k}"], writes=[f"B{par}"])
                p.op("v", lambda v, o=o, a=a, w=w, par=par: v.tensor_copy(
                    qev[par][:, 0:w], PM[par][:, 0:w]),
                    reads=[f"B{par}"], writes=[f"xsq{par}"])
                p.op("y", lambda y, o=o, a=a, w=w, par=par, base=base: y.dma_start(
                    out=qkv_sp[o][:, base + a:base + a + w],
                    in_=qev[par][:, 0:w]),
                    reads=[f"xsq{par}"], writes=[f"qsp:{o}:{h}"], is_dma=True)

    _l1_era_c(nc, p, ar, locals())
    p.op("s", lambda s: None, reads=["OUT"], tag="fence")
    p.emit(es)
    return nc, es


def _l1_era_c(nc, p, ar, env):
    g = env
    qkv_sp, qblob, kblob, vblob = g["qkv_sp"], g["qblob"], g["kblob"], g["vblob"]
    qc1_t, kc1_t, qc2_t, kc2_t = g["qc1_t"], g["kc1_t"], g["qc2_t"], g["kc2_t"]
    qc1b, kc1b, qc2b, kc2b = g["qc1b"], g["kc1b"], g["qc2b"], g["kc2b"]
    combm, eq_ul, ek_ul = g["combm"], g["eq_ul"], g["ek_ul"]
    l2q, l2k, erq, erk = g["l2q"], g["l2k"], g["erq"], g["erk"]
    cos_in, sin_in, kt_rep = g["cos_in"], g["sin_in"], g["kt_rep"]
    identbf, halosb = g["identbf"], g["halosb"]
    P0, P1, P2, P3 = g["P0"], g["P1"], g["P2"], g["P3"]
    P4, P5, P7 = g["P4"], g["P5"], g["P7"]
    PM = [P0, P1]

    p.barrier()
    ar.reset()
    c1q = ar.take(128, 8 * STRIP, BF16)
    c1k = ar.take(128, 2 * STRIP, BF16)
    c2q = ar.take(128, 8 * TPC, BF16)
    c2k = ar.take(128, 2 * TPC, BF16)
    qst = [ar.take(128, STRIP, BF16) for _ in range(2)]
    qmsb = ar.take(64, TPC, BF16)
    kmsb = ar.take(64, TPC, BF16)
    rq_f = ar.take(16, TPC, F32)
    rqbf = ar.take(16, TPC, BF16)
    rk_f = ar.take(4, TPC, F32)
    rkbf = ar.take(4, TPC, BF16)
    cosT = ar.take(128, TPC, F32)
    sinT = ar.take(128, TPC, F32)
    ce = ar.take(128, 512, F32)
    se_ = ar.take(128, 512, F32)
    m1 = ar.take(128, 512, F32)
    m2 = ar.take(128, 512, F32)
    qup = ar.take(128, 512, BF16)
    qlo = ar.take(128, 512, BF16)
    vstg = [ar.take(128, 128, BF16) for _ in range(2)]
    cw = [ar.take(128, 3072, BF16) for _ in range(2)]
    qc1bf = ar.take(128, 3072, BF16)
    kc1bf = ar.take(128, 768, BF16)
    kc2bf = ar.take(128, 1536, BF16)
    combbf = ar.take(128, 64, BF16)
    eqbf = ar.take(64, 256, BF16)
    ekbf = ar.take(64, 256, BF16)
    l2qbf = ar.take(128, 128, BF16)
    l2kbf = ar.take(128, 4, BF16)
    erqbf = ar.take(16, 512, BF16)
    erkbf = ar.take(4, 128, BF16)
    nsq = ar.take(16, 512, F32)
    sqv = [ar.take(128, 512, BF16) for _ in range(2)]
    kt_sb = ar.take(128, 1, F32)
    b1q = ar.take(128, 8, F32)
    b1k = ar.take(128, 2, F32)
    b2q = ar.take(128, 8, F32)
    b2k = ar.take(128, 2, F32)

    p.op("g", lambda gg: gg.dma_start(out=qc1bf[:, :],
         in_=_rear(qc1_t, [(128, 128), (49152, 8), (16384, 3), (1, 128)])),
         writes=["qc1bf"], is_dma=True)
    p.op("g", lambda gg: gg.dma_start(out=kc1bf[:, :],
         in_=_rear(kc1_t, [(128, 128), (49152, 2), (16384, 3), (1, 128)])),
         writes=["kc1bf"], is_dma=True)
    p.op("g", lambda gg: gg.dma_start(out=kc2bf[:, :],
         in_=_rear(kc2_t, [(128, 128), (49152, 4), (16384, 3), (1, 128)])),
         writes=["kc2bf"], is_dma=True)
    p.op("g", lambda gg: gg.dma_start(out=combbf[:, :], in_=combm),
         writes=["combbf"], is_dma=True)
    p.op("g", lambda gg: gg.dma_start(out=eqbf[:, :],
         in_=_rear(eq_ul, [(128, 64), (8192, 2), (1, 128)])),
         writes=["eqbf"], is_dma=True)
    p.op("g", lambda gg: gg.dma_start(out=ekbf[:, :],
         in_=_rear(ek_ul, [(128, 64), (8192, 2), (1, 128)])),
         writes=["ekbf"], is_dma=True)
    p.op("g", lambda gg: gg.dma_start(out=l2qbf[:, :],
         in_=_rear(l2q, [(16, 128), (2048, 8), (1, 16)])),
         writes=["l2qbf"], is_dma=True)
    p.op("g", lambda gg: gg.dma_start(out=l2kbf[:, :], in_=l2k),
         writes=["l2kbf"], is_dma=True)
    p.op("g", lambda gg: gg.dma_start(out=erqbf[:, :],
         in_=_rear(erq, [(128, 16), (2048, 4), (1, 128)])),
         writes=["erqbf"], is_dma=True)
    p.op("g", lambda gg: gg.dma_start(out=erkbf[:, :], in_=erk),
         writes=["erkbf"], is_dma=True)
    p.op("y", lambda y: y.dma_start(out=cosT[:, :], in_=cos_in),
         writes=["cosT"], is_dma=True)
    p.op("y", lambda y: y.dma_start(out=sinT[:, :], in_=sin_in),
         writes=["sinT"], is_dma=True)
    p.op("y", lambda y: y.dma_start(out=b1q[:, :], in_=qc1b),
         writes=["b1q"], is_dma=True)
    p.op("y", lambda y: y.dma_start(out=b1k[:, :], in_=kc1b),
         writes=["b1k"], is_dma=True)
    p.op("y", lambda y: y.dma_start(out=b2q[:, :], in_=qc2b),
         writes=["b2q"], is_dma=True)
    p.op("y", lambda y: y.dma_start(out=b2k[:, :], in_=kc2b),
         writes=["b2k"], is_dma=True)
    p.op("y", lambda y: y.dma_start(out=kt_sb[:, :], in_=kt_rep),
         writes=["kt_sb"], is_dma=True)

    C1SUBS = [(2, 4), (4, 516), (516, 1028), (1028, 1540), (1540, 2052)]
    C2SUBS = [(4, 516), (516, 1028), (1028, 1540), (1540, 2052)]

    def qkv_stream(c, par):
        p.op("y", lambda y, c=c, par=par: y.dma_start(
            out=qst[par][:, :], in_=qkv_sp[c]),
            reads=[f"qsp:{c}:0", f"qsp:{c}:1"], writes=[f"qst{par}"],
            is_dma=True)

    def conv1(nchunk, wsrc, wkey, bias, bkey, dst, dstkey, gi0):
        for c in range(nchunk):
            par = c % 2
            qkv_stream(c if dstkey == "c1q" else 8 + c, par)
            for si, (a, b) in enumerate(C1SUBS):
                w = b - a
                pp = (gi0 + c * len(C1SUBS) + si) % 2
                for tap in range(3):
                    p.op("t", lambda t, c=c, tap=tap, a=a, w=w, par=par, pp=pp: t.matmul(
                        PM[pp][:, 0:w],
                        wsrc[:, (c * 3 + tap) * 128:(c * 3 + tap) * 128 + 128],
                        qst[par][:, a - 2 + tap:a - 2 + tap + w],
                        start=(tap == 0), stop=(tap == 2)),
                        reads=[wkey, f"qst{par}"], writes=[f"B{pp}"])
                def ev(v, c=c, a=a, w=w, pp=pp):
                    return v.tensor_scalar(
                        out=dst[:, c * STRIP + a:c * STRIP + a + w],
                        in0=PM[pp][:, 0:w], scalar1=bias[:, c:c + 1],
                        scalar2=None, op0=ALU.add)
                p.op("v", ev, reads=[f"B{pp}", bkey], writes=[dstkey])
            def hz(v, c=c):
                return v.tensor_scalar(
                    out=dst[:, c * STRIP + 2:c * STRIP + 4],
                    in0=dst[:, c * STRIP + 2:c * STRIP + 4],
                    scalar1=halosb[:, :], scalar2=None, op0=ALU.mult)
            p.op("v", hz, reads=["halosb", dstkey], writes=[dstkey])
        return gi0 + nchunk * len(C1SUBS)

    gi = 0
    gi = conv1(8, qc1bf, "qc1bf", b1q, "b1q", c1q, "c1q", gi)
    gi = conv1(2, kc1bf, "kc1bf", b1k, "b1k", c1k, "c1k", gi)

    for si, (a, b) in enumerate(C2SUBS):
        w = b - a
        for c in range(8):
            par = c % 2
            qkv_stream(c, par)
            p.op("t", lambda t, c=c, a=a, w=w, par=par: t.matmul(
                P2[0:64, 0:w], combbf[:, :], qst[par][:, a:a + w],
                start=(c == 0), stop=(c == 7)),
                reads=["combbf", f"qst{par}"], writes=["B2"])
        p.op("v", lambda v, a=a, w=w: v.tensor_copy(
            qmsb[:, a - 4:a - 4 + w], P2[0:64, 0:w]),
            reads=["B2"], writes=["qmsb"])
        for c in range(2):
            par = c % 2
            qkv_stream(8 + c, par)
            p.op("t", lambda t, c=c, a=a, w=w, par=par: t.matmul(
                P3[0:64, 0:w], combbf[:, :], qst[par][:, a:a + w],
                start=(c == 0), stop=(c == 1)),
                reads=["combbf", f"qst{par}"], writes=["B3"])
        p.op("v", lambda v, a=a, w=w: v.tensor_copy(
            kmsb[:, a - 4:a - 4 + w], P3[0:64, 0:w]),
            reads=["B3"], writes=["kmsb"])

    for u in range(8):
        wpar = u % 2
        p.op("g", lambda gg, u=u, wpar=wpar: gg.dma_start(
            out=cw[wpar][:, :],
            in_=_rear(qc2_t, [(128, 128), (49152, 8), (16384, 3), (1, 128)],
                      offset=u * 8 * 3 * 128 * 128)),
            writes=[f"cw{wpar}"], is_dma=True)
        for si, (a, b) in enumerate(C2SUBS):
            w = b - a
            pp = (u * len(C2SUBS) + si) % 2
            for k in range(8):
                for tap in range(3):
                    p.op("t", lambda t, u=u, k=k, tap=tap, a=a, w=w, pp=pp, wpar=wpar: t.matmul(
                        PM[pp][:, 0:w],
                        cw[wpar][:, (k * 3 + tap) * 128:(k * 3 + tap) * 128 + 128],
                        c1q[:, k * STRIP + a - 2 + tap:k * STRIP + a - 2 + tap + w],
                        start=(k == 0 and tap == 0), stop=False),
                        reads=[f"cw{wpar}", "c1q"], writes=[f"B{pp}"])
            lu = 0 if u < 4 else 1
            p.op("t", lambda t, u=u, a=a, w=w, pp=pp, lu=lu: t.matmul(
                PM[pp][:, 0:w], eqbf[:, lu * 128:lu * 128 + 128],
                kmsb[:, a - 4:a - 4 + w], start=False, stop=True),
                reads=["eqbf", "kmsb"], writes=[f"B{pp}"])
            p.op("v", lambda v, u=u, a=a, w=w, pp=pp: v.tensor_scalar(
                out=c2q[:, u * TPC + a - 4:u * TPC + a - 4 + w],
                in0=PM[pp][:, 0:w], scalar1=b2q[:, u:u + 1],
                scalar2=None, op0=ALU.add),
                reads=[f"B{pp}", "b2q"], writes=["c2q"])

    for u in range(2):
        for si, (a, b) in enumerate(C2SUBS):
            w = b - a
            pp = (u * len(C2SUBS) + si) % 2
            for k in range(2):
                for tap in range(3):
                    p.op("t", lambda t, u=u, k=k, tap=tap, a=a, w=w, pp=pp: t.matmul(
                        PM[pp][:, 0:w],
                        kc2bf[:, ((u * 2 + k) * 3 + tap) * 128:((u * 2 + k) * 3 + tap) * 128 + 128],
                        c1k[:, k * STRIP + a - 2 + tap:k * STRIP + a - 2 + tap + w],
                        start=(k == 0 and tap == 0), stop=False),
                        reads=["kc2bf", "c1k"], writes=[f"B{pp}"])
            p.op("t", lambda t, u=u, a=a, w=w, pp=pp: t.matmul(
                PM[pp][:, 0:w], ekbf[:, u * 128:u * 128 + 128],
                qmsb[:, a - 4:a - 4 + w], start=False, stop=True),
                reads=["ekbf", "qmsb"], writes=[f"B{pp}"])
            p.op("v", lambda v, u=u, a=a, w=w, pp=pp: v.tensor_scalar(
                out=c2k[:, u * TPC + a - 4:u * TPC + a - 4 + w],
                in0=PM[pp][:, 0:w], scalar1=b2k[:, u:u + 1],
                scalar2=None, op0=ALU.add),
                reads=[f"B{pp}", "b2k"], writes=["c2k"])

    xsq = sqv
    for si, (a, b) in enumerate(C2SUBS):
        w = b - a
        ra = a - 4
        for u in range(8):
            par = u % 2
            p.op("s", lambda s, u=u, ra=ra, w=w, par=par: s.activation(
                out=xsq[par][:, 0:w], in_=c2q[:, u * TPC + ra:u * TPC + ra + w],
                func=AF.Square),
                reads=["c2q"], writes=[f"xsq{par}"])
            p.op("t", lambda t, u=u, w=w, par=par: t.matmul(
                P2[0:16, 0:w], l2qbf[:, u * 16:u * 16 + 16], xsq[par][:, 0:w],
                start=(u == 0), stop=(u == 7)),
                reads=["l2qbf", f"xsq{par}"], writes=["B2"])
        def ev_rq(v, w=w, ra=ra):
            return v.tensor_scalar(out=nsq[0:16, 0:w], in0=P2[0:16, 0:w],
                                   scalar1=1e-24, scalar2=None, op0=ALU.max)
        p.op("v", ev_rq, reads=["B2"], writes=["nsq"])
        p.op("s", lambda s, w=w: s.activation(
            out=nsq[0:16, 0:w], in_=nsq[0:16, 0:w], func=AF.Ln),
            reads=["nsq"], writes=["nsq"])
        p.op("s", lambda s, w=w, ra=ra: s.activation(
            out=rq_f[0:16, ra:ra + w], in_=nsq[0:16, 0:w],
            func=AF.Exp, scale=-0.5),
            reads=["nsq"], writes=["rq_f"])
        p.op("v", lambda v, w=w, ra=ra: v.tensor_copy(
            rqbf[:, ra:ra + w], rq_f[:, ra:ra + w]),
            reads=["rq_f"], writes=["rqbf"])
        for u in range(2):
            par = u % 2
            p.op("s", lambda s, u=u, ra=ra, w=w, par=par: s.activation(
                out=xsq[par][:, 0:w], in_=c2k[:, u * TPC + ra:u * TPC + ra + w],
                func=AF.Square),
                reads=["c2k"], writes=[f"xsq{par}"])
            p.op("t", lambda t, u=u, w=w, par=par: t.matmul(
                P3[0:4, 0:w], l2kbf[:, :], xsq[par][:, 0:w],
                start=(u == 0), stop=(u == 1)),
                reads=["l2kbf", f"xsq{par}"], writes=["B3"])
        def ev_rk(v, w=w):
            return v.tensor_scalar(out=nsq[0:4, 0:w], in0=P3[0:4, 0:w],
                                   scalar1=1e-24, scalar2=None, op0=ALU.max)
        p.op("v", ev_rk, reads=["B3"], writes=["nsq"])
        p.op("s", lambda s, w=w: s.activation(
            out=nsq[0:4, 0:w], in_=nsq[0:4, 0:w], func=AF.Ln),
            reads=["nsq"], writes=["nsq"])
        p.op("s", lambda s, w=w, ra=ra: s.activation(
            out=rk_f[0:4, ra:ra + w], in_=nsq[0:4, 0:w],
            func=AF.Exp, scale=-0.5),
            reads=["nsq"], writes=["rk_f"])
        p.op("v", lambda v, w=w, ra=ra: v.tensor_scalar(
            out=rk_f[0:4, ra:ra + w], in0=rk_f[0:4, ra:ra + w],
            scalar1=kt_sb[0:4, :], scalar2=None, op0=ALU.mult),
            reads=["rk_f", "kt_sb"], writes=["rk_f"])
        p.op("v", lambda v, w=w, ra=ra: v.tensor_copy(
            rkbf[:, ra:ra + w], rk_f[:, ra:ra + w]),
            reads=["rk_f"], writes=["rkbf"])

    def rope(npair, erbf, erkey, ernrows, rbf_t, rkey, src, srckey, stride_u,
             blob, blobname):
        for u in range(npair):
            for si, (a, b) in enumerate(C2SUBS):
                w = b - a
                ra = a - 4
                p.op("t", lambda t, u=u, w=w, ra=ra: t.matmul(
                    P4[:, 0:w], erbf[0:ernrows, u * 128:u * 128 + 128] if npair > 1
                    else erbf[0:ernrows, 0:128],
                    rbf_t[:, ra:ra + w], start=True, stop=True),
                    reads=[erkey, rkey], writes=["B4"])
                up_lo = (u * stride_u + ra, (npair + u) * stride_u + ra)
                p.op("v", lambda v, w=w, ra=ra: v.tensor_tensor(
                    out=ce[:, 0:w], in0=P4[:, 0:w], in1=cosT[:, ra:ra + w],
                    op=ALU.mult), reads=["B4", "cosT"], writes=["ce"])
                p.op("v", lambda v, w=w, ra=ra: v.tensor_tensor(
                    out=se_[:, 0:w], in0=P4[:, 0:w], in1=sinT[:, ra:ra + w],
                    op=ALU.mult), reads=["B4", "sinT"], writes=["se"])
                p.op("v", lambda v, w=w, a0=up_lo[0]: v.tensor_tensor(
                    out=m1[:, 0:w], in0=src[:, a0:a0 + w], in1=ce[:, 0:w],
                    op=ALU.mult), reads=["ce", srckey], writes=["m1"])
                p.op("v", lambda v, w=w, a1=up_lo[1]: v.tensor_tensor(
                    out=m2[:, 0:w], in0=src[:, a1:a1 + w], in1=se_[:, 0:w],
                    op=ALU.mult), reads=["se", srckey], writes=["m2"])
                p.op("v", lambda v, w=w: v.tensor_sub(
                    qup[:, 0:w], m1[:, 0:w], m2[:, 0:w]),
                    reads=["m1", "m2"], writes=["qup"])
                p.op("v", lambda v, w=w, a1=up_lo[1]: v.tensor_tensor(
                    out=m1[:, 0:w], in0=src[:, a1:a1 + w], in1=ce[:, 0:w],
                    op=ALU.mult), reads=["ce", srckey], writes=["m1"])
                p.op("v", lambda v, w=w, a0=up_lo[0]: v.tensor_tensor(
                    out=m2[:, 0:w], in0=src[:, a0:a0 + w], in1=se_[:, 0:w],
                    op=ALU.mult), reads=["se", srckey], writes=["m2"])
                p.op("v", lambda v, w=w: v.tensor_add(
                    qlo[:, 0:w], m1[:, 0:w], m2[:, 0:w]),
                    reads=["m1", "m2"], writes=["qlo"])
                p.op("y", lambda y, u=u, w=w, ra=ra: y.dma_start(
                    out=blob[u][:, ra:ra + w], in_=qup[:, 0:w]),
                    reads=["qup"], writes=["OUT"], is_dma=True)
                p.op("y", lambda y, u=u, w=w, ra=ra: y.dma_start(
                    out=blob[npair + u][:, ra:ra + w], in_=qlo[:, 0:w]),
                    reads=["qlo"], writes=["OUT"], is_dma=True)

    rope(4, erqbf, "erqbf", 16, rqbf, "rqbf", c2q, "c2q", TPC, qblob, "q")
    rope(1, erkbf, "erkbf", 4, rkbf, "rkbf", c2k, "c2k", TPC, kblob, "k")

    for half_ch, (chunk, shift, colbase) in enumerate([(10, 0, 0), (11, -1, 128)]):
        par0 = half_ch % 2
        qkv_stream(chunk, par0)
        for t16 in range(16):
            vp = t16 % 2
            p.op("t", lambda t, t16=t16, shift=shift, par0=par0, vp=vp: t.transpose(
                P7[:, :], qst[par0][:, 4 + 128 * t16 + shift:4 + 128 * t16 + shift + 128],
                identbf[:, :]),
                reads=[f"qst{par0}", "identbf"], writes=["B7"])
            p.op("v", lambda v, vp=vp: v.tensor_copy(vstg[vp][:, :], P7[:, :]),
                 reads=["B7"], writes=[f"vstg{vp}"])
            p.op("y", lambda y, t16=t16, colbase=colbase, vp=vp: y.dma_start(
                out=vblob[128 * t16:128 * t16 + 128, colbase:colbase + 128],
                in_=vstg[vp][:, :]),
                reads=[f"vstg{vp}"], writes=["OUT"], is_dma=True)





def _prep_core_l2(l1_results, ins, c):
    s, g = c // 4, c % 4
    f32 = np.float32
    katt = np.zeros((128, S), BF)
    qatt = np.zeros((2, 128, S), BF)
    vgrid = np.ones((128, 65 * 64), BF)
    for blk in range(4):
        rr = l1_results[4 * s + blk]
        cols = slice(TPC * blk, TPC * (blk + 1))
        kb, qb, vb = rr["kblob"], rr["qblob"], rr["vblob"]
        katt[0:32, cols] = kb[0, 32 * g:32 * g + 32]
        katt[32:64, cols] = kb[1, 32 * g:32 * g + 32]
        katt[64:96, cols] = kb[0, 32 * g:32 * g + 32]
        katt[96:128, cols] = kb[1, 32 * g:32 * g + 32]
        for i in range(2):
            for jj in range(2):
                h_local = 2 * i + jj
                qatt[i][64 * jj:64 * jj + 32, cols] = qb[g][32 * h_local:32 * h_local + 32]
                qatt[i][64 * jj + 32:64 * jj + 64, cols] = qb[4 + g][32 * h_local:32 * h_local + 32]
        vloc = vb[:, 64 * g:64 * g + 64]
        for kc_loc in range(16):
            kc = 16 * blk + kc_loc
            vgrid[:, 65 * kc:65 * kc + 64] = vloc[128 * kc_loc:128 * kc_loc + 128]
    return {"katt": katt, "qatt": qatt, "vgrid": vgrid,
            "kt_rep": np.repeat(np.asarray(ins["L0_attn_key_temp"], f32).reshape(1, 1),
                                128, axis=0)}


def build_launch2():
    nc = bass.Bass("TRN2", target_bir_lowering=False)
    es = ExitStack()
    p = Prog(nc)

    katt_i = nc.dram_tensor("katt", [128, S], BF16, kind="ExternalInput").ap()
    qatt_i = nc.dram_tensor("qatt", [2, 128, S], BF16, kind="ExternalInput").ap()
    vgrid_i = nc.dram_tensor("vgrid", [128, 65 * 64], BF16, kind="ExternalInput").ap()
    kt_rep = nc.dram_tensor("kt_rep", [128, 1], F32, kind="ExternalInput").ap()
    oblob = nc.dram_tensor("oblob", [4, 64, S], BF16, kind="ExternalOutput").ap()

    katt = _sb(nc, es, "katt_s", [128, S], BF16)
    qatt = [_sb(nc, es, f"qatt_s{i}", [128, S], BF16) for i in range(2)]
    vgrid = _sb(nc, es, "vgrid_s", [128, 65 * 64], BF16)
    negsh = _sb(nc, es, "negsh", [128, 1], F32)
    onesf = _sb(nc, es, "onesf", [1, 64], F32)
    ptile = _sb(nc, es, "ptile", [128, 2048], BF16)
    rr_sb = _sb(nc, es, "rr_sb", [1, 512], F32)
    rb_sb = _sb(nc, es, "rb_sb", [64, 512], F32)
    obf = _sb(nc, es, "obf", [64, 512], BF16)

    PS = es.enter_context(nc.psum_tensor("PS", [128, 2048], F32))
    PO = [es.enter_context(nc.psum_tensor(f"PO{j}", [65, 512], F32))
          for j in range(4)]

    p.op("g", lambda g: g.dma_start(out=katt[:, :], in_=katt_i),
         writes=["katt"], is_dma=True)
    for i in range(2):
        p.op("g", lambda g, i=i: g.dma_start(out=qatt[i][:, :], in_=qatt_i[i]),
             writes=[f"qatt{i}"], is_dma=True)
    p.op("g", lambda g: g.dma_start(out=vgrid[:, :], in_=vgrid_i),
         writes=["vgrid"], is_dma=True)
    p.op("y", lambda y: y.dma_start(out=negsh[:, :], in_=kt_rep),
         writes=["negsh"], is_dma=True)
    p.op("s", lambda s: s.activation(out=negsh[:, :], in_=negsh[:, :],
                                     func=AF.Abs, scale=-0.125),
         reads=["negsh"], writes=["negsh"])
    p.op("v", lambda v: v.tensor_scalar(out=negsh[:, :], in0=negsh[:, :],
                                        scalar1=-1.0, scalar2=None,
                                        op0=ALU.mult),
         reads=["negsh"], writes=["negsh"])
    p.op("v", lambda v: v.memset(onesf[:, :], 1.0), writes=["onesf"])

    for qt in range(16):
        q0 = 512 * qt
        nkc = 4 * qt + 4
        for kc in range(nkc):
            for i in range(2):
                for half in range(2):
                    j = 2 * i + half
                    pb = 64 * half
                    p.op("t", lambda t, i=i, half=half, j=j, pb=pb, kc=kc, q0=q0: t.matmul(
                        PS[:, 512 * j:512 * j + 512],
                        katt[pb:pb + 64, 128 * kc:128 * kc + 128],
                        qatt[i][pb:pb + 64, q0:q0 + 512],
                        start=True, stop=True),
                        reads=["katt", f"qatt{i}"],
                        writes=[f"S{j}"], tag=f"qk{qt}:{kc}")
            p.op("s", lambda s, kc=kc: s.activation(
                out=ptile[:, :], in_=PS[:, :], func=AF.Exp,
                bias=negsh[:, :], scale=0.125),
                reads=["S0", "S1", "S2", "S3", "negsh"], writes=["ptile"],
                tag=f"exp{qt}:{kc}")
            if kc >= 4 * qt:
                p.op("g", lambda g, kc=kc, q0=q0: g.affine_select(
                    out=ptile[:, :], in_=ptile[:, :],
                    compare_op=ALU.is_ge, fill=0.0,
                    base=q0 - 128 * kc, pattern=[[0, 4], [1, 512]],
                    channel_multiplier=-1),
                    reads=["ptile"], writes=["ptile"], tag=f"mask{qt}:{kc}")
            for j in range(4):
                p.op("t", lambda t, j=j, kc=kc: t.matmul(
                    PO[j][:, :], vgrid[:, 65 * kc:65 * kc + 65],
                    ptile[:, 512 * j:512 * j + 512],
                    start=(kc == 0), stop=(kc == nkc - 1)),
                    reads=["vgrid", "ptile"],
                    writes=[f"O{j}"], tag=f"av{qt}:{kc}:{j}")
        for j in range(4):
            p.op("v", lambda v, j=j: v.reciprocal(rr_sb[:, :], PO[j][64:65, :]),
                 reads=[f"O{j}"], writes=["rr"])
            p.op("t", lambda t: t.matmul(
                PS[0:64, 0:512], onesf[:, :], rr_sb[:, :],
                start=True, stop=True),
                reads=["rr", "onesf"], writes=["S0"])
            p.op("v", lambda v: v.tensor_copy(rb_sb[:, :], PS[0:64, 0:512]),
                 reads=["S0"], writes=["rb"])
            p.op("v", lambda v, j=j: v.tensor_tensor(
                out=obf[:, :], in0=PO[j][0:64, :], in1=rb_sb[:, :],
                op=ALU.mult),
                reads=[f"O{j}", "rb"], writes=["obf"])
            p.op("y", lambda y, j=j, q0=q0: y.dma_start(
                out=oblob[j][:, q0:q0 + 512], in_=obf[:, :]),
                reads=["obf"], writes=["OUT"], is_dma=True)
    p.op("s", lambda s: None, reads=["OUT"], tag="fence")
    p.emit(es)
    return nc, es



def _prep_core_l3(l2_results, ins, sh, c):
    s, blk = c // 4, c % 4
    t0 = TPC * blk
    olocal = np.zeros((8, 128, TPC), BF)
    for i in range(8):
        g, pair = i // 2, i % 2
        ob = l2_results[4 * s + g]["oblob"]
        olocal[i, 0:64] = ob[2 * pair][:, t0:t0 + TPC]
        olocal[i, 64:128] = ob[2 * pair + 1][:, t0:t0 + TPC]
    d = {"olocal": olocal}
    for k in ("wo_t", "wup_t", "ag36", "lorab_t", "eb4", "ones_row",
              "exp4to32", "ident"):
        d[k] = sh[k]
    return d


H3 = 1024
SUBS3 = [(0, 512), (512, 1024)]


def build_launch3():
    nc = bass.Bass("TRN2", target_bir_lowering=False)
    es = ExitStack()
    p = Prog(nc)

    def din(name, shape, dt=F32):
        return nc.dram_tensor(name, shape, dt, kind="ExternalInput").ap()

    olocal = nc.dram_tensor("olocal", [8, 128, TPC], BF16, kind="ExternalInput").ap()
    x1_in = din("x1_in", [32, 128, TPC])
    wo_t = din("wo_t", [32, 8, 128, 128])
    wup_t = din("wup_t", [32, 32, 128, 128])
    ag36 = din("ag36", [32, 128, 36])
    lorab_t = din("lorab_t", [32, 32, 128])
    eb4 = din("eb4", [4, 1])
    ones_row = din("ones_row", [1, 128])
    exp4to32 = din("exp4to32", [4, 32])
    ident = din("ident", [128, 128])
    out_fm = nc.dram_tensor("out_fm", [32, 128, TPC], F32, kind="ExternalOutput").ap()
    x2_sp = nc.dram_tensor("x2_sp", [32, 128, TPC], F32, kind="Internal").ap()

    eb4sb = _sb(nc, es, "eb4sb", [4, 1], F32)
    onesrbf = _sb(nc, es, "onesrbf", [1, 128], BF16)
    identbf = _sb(nc, es, "identbf", [128, 128], BF16)
    e432bf = _sb(nc, es, "e432bf", [4, 32], BF16)
    onescbf_v = _sb(nc, es, "onescbf", [128, 1], BF16)

    P0 = _ps(nc, es, "P0", [128, 512])
    P1 = _ps(nc, es, "P1", [128, 512])
    P2 = _ps(nc, es, "P2", [128, 512])
    P3 = _ps(nc, es, "P3", [128, 512])
    P4 = _ps(nc, es, "P4", [128, 512])
    P5 = _ps(nc, es, "P5", [128, 512])
    P7 = _ps(nc, es, "P7", [128, 128], BF16)
    PM = [P0, P1]

    ar = Arena(nc, es, 206000)
    AR1 = ar.take(128, 32 * H3, BF16)
    AR2 = ar.take(128, 32 * H3, BF16)
    ag36bf = ar.take(128, 32 * 36, BF16)
    lorabbf = ar.take(32, 32 * 128, BF16)
    olh = AR2[:, 0:8 * H3]
    wt = [ar.take(128, 4096, BF16) for _ in range(2)]
    xsq = [ar.take(128, H3, BF16) for _ in range(2)]
    tmpc = xsq[0]
    mu128 = ar.take(128, H3, BF16)
    r128 = ar.take(128, H3, BF16)
    rowA = ar.take(1, H3, F32)
    rowB = ar.take(1, H3, F32)
    mubf = ar.take(1, H3, BF16)
    rbf = ar.take(1, H3, BF16)
    zsb = ar.take(32, H3, F32)
    zm = ar.take(32, H3, BF16)
    lsb = ar.take(4, H3, F32)
    rowC = lsb[0:1, 0:H3]
    lsbbf = ar.take(4, H3, BF16)
    tmsb = ar.take(128, 32, F32)
    mxsb = ar.take(128, 8, F32)
    gesb = ar.take(128, 32, F32)
    ohsb = ar.take(128, 32, F32)
    notsb = ar.take(128, 8, F32)
    ohbf = ar.take(128, 32, BF16)
    ohfm = ar.take(4, H3, BF16)
    x2f = [ar.take(128, 512, F32) for _ in range(2)]
    xres = [ar.take(128, 512, F32) for _ in range(2)]

    def A1(k):
        return AR1[:, H3 * k:H3 * (k + 1)]

    def A2(k):
        return AR2[:, H3 * k:H3 * (k + 1)]

    p.op("g", lambda g: g.dma_start(out=ag36bf[:, :],
         in_=_rear(ag36, [(36, 128), (4608, 32), (1, 36)])),
         writes=["ag36bf"], is_dma=True)
    p.op("g", lambda g: g.dma_start(out=lorabbf[:, :],
         in_=_rear(lorab_t, [(128, 32), (4096, 32), (1, 128)])),
         writes=["lorabbf"], is_dma=True)
    p.op("g", lambda g: g.dma_start(out=onesrbf[:, :], in_=ones_row),
         writes=["onesrbf"], is_dma=True)
    p.op("g", lambda g: g.dma_start(out=identbf[:, :], in_=ident),
         writes=["identbf"], is_dma=True)
    p.op("g", lambda g: g.dma_start(out=e432bf[:, :], in_=exp4to32),
         writes=["e432bf"], is_dma=True)
    p.op("y", lambda y: y.dma_start(out=eb4sb[:, :], in_=eb4),
         writes=["eb4sb"], is_dma=True)
    p.op("v", lambda v: v.memset(onescbf_v[:, :], 1.0), writes=["onescbf"])

    def stats3(src_of_k, key_of_k):
        PSX = [P2, P3]
        PSQ = [P4, P5]
        for k in range(32):
            par = k % 2
            p.op("s", lambda s, k=k, par=par: s.activation(
                out=xsq[par][:, :], in_=src_of_k(k), func=AF.Square),
                reads=[key_of_k(k)], writes=[f"xsq{par}"])
            for si, (a, b) in enumerate(SUBS3):
                w = b - a
                p.op("t", lambda t, k=k, a=a, w=w, si=si: t.matmul(
                    PSX[si][0:1, 0:w], onescbf_v[:, :], src_of_k(k)[:, a:a + w],
                    start=(k == 0), stop=(k == 31)),
                    reads=[key_of_k(k), "onescbf"], writes=[("B2", "B3")[si]])
                p.op("t", lambda t, k=k, a=a, w=w, par=par, si=si: t.matmul(
                    PSQ[si][0:1, 0:w], onescbf_v[:, :], xsq[par][:, a:a + w],
                    start=(k == 0), stop=(k == 31)),
                    reads=[f"xsq{par}", "onescbf"], writes=[("B4", "B5")[si]])
        def ev_rows(v):
            v.tensor_copy(rowA[0:1, 0:512], PSX[0][0:1, 0:512])
            v.tensor_copy(rowA[0:1, 512:1024], PSX[1][0:1, 0:512])
            v.tensor_copy(rowB[0:1, 0:512], PSQ[0][0:1, 0:512])
            return v.tensor_copy(rowB[0:1, 512:1024], PSQ[1][0:1, 0:512])
        p.op("v", ev_rows, reads=["B2", "B3", "B4", "B5"],
             writes=["rowA", "rowB"])
        p.op("v", lambda v: v.tensor_scalar(
            out=rowA[:, :], in0=rowA[:, :], scalar1=1.0 / 4096,
            scalar2=None, op0=ALU.mult), reads=["rowA"], writes=["rowA"])
        p.op("v", lambda v: v.tensor_scalar(
            out=rowB[:, :], in0=rowB[:, :], scalar1=1.0 / 4096,
            scalar2=None, op0=ALU.mult), reads=["rowB"], writes=["rowB"])
        p.op("v", lambda v: v.tensor_tensor(
            out=rowC[:, :], in0=rowA[:, :], in1=rowA[:, :], op=ALU.mult),
            reads=["rowA"], writes=["lsb"])
        p.op("v", lambda v: v.tensor_tensor(
            out=rowB[:, :], in0=rowB[:, :], in1=rowC[:, :], op=ALU.subtract),
            reads=["rowB", "lsb"], writes=["rowB"])
        p.op("v", lambda v: v.tensor_scalar(
            out=rowB[:, :], in0=rowB[:, :], scalar1=EPS, scalar2=None,
            op0=ALU.add), reads=["rowB"], writes=["rowB"])
        p.op("s", lambda s: s.activation(out=rowC[:, :], in_=rowB[:, :],
                                         func=AF.Ln),
             reads=["rowB"], writes=["lsb"])
        p.op("s", lambda s: s.activation(out=rowC[:, :], in_=rowC[:, :],
                                         func=AF.Exp, scale=-0.5),
             reads=["lsb"], writes=["lsb"])
        p.op("v", lambda v: v.tensor_copy(mubf[:, :], rowA[:, :]),
             reads=["rowA"], writes=["mubf"])
        p.op("v", lambda v: v.tensor_copy(rbf[:, :], rowC[:, :]),
             reads=["lsb"], writes=["rbf"])
        for si, (a, b) in enumerate(SUBS3):
            w = b - a
            p.op("t", lambda t, a=a, w=w: t.matmul(
                P4[:, 0:w], onesrbf[0:1, :], mubf[0:1, a:a + w],
                start=True, stop=True),
                reads=["mubf", "onesrbf"], writes=["B4"])
            p.op("t", lambda t, a=a, w=w: t.matmul(
                P5[:, 0:w], onesrbf[0:1, :], rbf[0:1, a:a + w],
                start=True, stop=True),
                reads=["rbf", "onesrbf"], writes=["B5"])
            p.op("v", lambda v, a=a, w=w: v.tensor_copy(
                mu128[:, a:a + w], P4[:, 0:w]), reads=["B4"], writes=["mu128"])
            p.op("v", lambda v, a=a, w=w: v.tensor_copy(
                r128[:, a:a + w], P5[:, 0:w]), reads=["B5"], writes=["r128"])

    def gate3():
        for k in range(32):
            for si, (a, b) in enumerate(SUBS3):
                dst = [P4, P5][si]
                dkey = ["B4", "B5"][si]
                w = b - a
                p.op("t", lambda t, k=k, a=a, w=w, dst=dst: t.matmul(
                    dst[0:36, 0:w], ag36bf[:, 36 * k:36 * (k + 1)],
                    A2(k)[:, a:a + w], start=(k == 0), stop=(k == 31)),
                    reads=[f"AR2:{k}", "ag36bf"], writes=[dkey])
        def ev_z(v):
            v.tensor_copy(zsb[:, 0:512], P4[0:32, 0:512])
            v.tensor_copy(zsb[:, 512:1024], P5[0:32, 0:512])
            v.tensor_copy(lsb[:, 0:512], P4[32:36, 0:512])
            return v.tensor_copy(lsb[:, 512:1024], P5[32:36, 0:512])
        p.op("v", ev_z, reads=["B4", "B5"], writes=["zsb", "lsb"])
        p.op("v", lambda v: v.tensor_scalar(
            out=lsb[:, :], in0=lsb[:, :], scalar1=eb4sb[:, :],
            scalar2=None, op0=ALU.add), reads=["lsb", "eb4sb"], writes=["lsb"])
        p.op("v", lambda v: v.tensor_copy(lsbbf[:, :], lsb[:, :]),
             reads=["lsb"], writes=["lsbbf"])
        for t8 in range(8):
            p.op("t", lambda t, t8=t8: t.transpose(
                P7[:, 4 * t8:4 * t8 + 4], lsbbf[0:4, 128 * t8:128 * t8 + 128],
                identbf[0:4, 0:4]),
                reads=["lsbbf", "identbf"], writes=["B7"])
        def ev_tm(v):
            tm3 = P7[:, 0:32].rearrange("p (t f) -> p t f", f=4)
            for e in range(4):
                v.tensor_copy(tmsb[:, 8 * e:8 * e + 8], tm3[:, :, e])
            return v.tensor_copy(mxsb[:, :], tm3[:, :, 0])
        p.op("v", ev_tm, reads=["B7"], writes=["tm", "mx"])
        p.op("v", lambda v: v.tensor_tensor(
            out=notsb[:, :], in0=tmsb[:, 0:8], in1=tmsb[:, 8:16], op=ALU.max),
            reads=["tm"], writes=["not"])
        p.op("v", lambda v: v.tensor_tensor(
            out=mxsb[:, :], in0=tmsb[:, 16:24], in1=tmsb[:, 24:32], op=ALU.max),
            reads=["tm"], writes=["mx"])
        p.op("v", lambda v: v.tensor_tensor(
            out=mxsb[:, :], in0=mxsb[:, :], in1=notsb[:, :], op=ALU.max),
            reads=["mx", "not"], writes=["mx"])
        for e in range(4):
            p.op("v", lambda v, e=e: v.tensor_tensor(
                out=gesb[:, 8 * e:8 * e + 8], in0=tmsb[:, 8 * e:8 * e + 8],
                in1=mxsb[:, :], op=ALU.is_ge),
                reads=["tm", "mx"], writes=["ge"])
        p.op("v", lambda v: v.tensor_copy(ohsb[:, 0:8], gesb[:, 0:8]),
             reads=["ge"], writes=["oh"])
        p.op("v", lambda v: v.tensor_scalar(
            out=notsb[:, :], in0=gesb[:, 0:8], scalar1=-1.0, scalar2=1.0,
            op0=ALU.mult, op1=ALU.add), reads=["ge"], writes=["not"])
        for e in range(1, 4):
            p.op("v", lambda v, e=e: v.tensor_tensor(
                out=ohsb[:, 8 * e:8 * e + 8], in0=gesb[:, 8 * e:8 * e + 8],
                in1=notsb[:, :], op=ALU.mult),
                reads=["ge", "not"], writes=["oh"])
            if e < 3:
                p.op("v", lambda v, e=e: v.tensor_scalar(
                    out=mxsb[:, :], in0=gesb[:, 8 * e:8 * e + 8],
                    scalar1=-1.0, scalar2=1.0, op0=ALU.mult, op1=ALU.add),
                    reads=["ge"], writes=["mx"])
                p.op("v", lambda v: v.tensor_tensor(
                    out=notsb[:, :], in0=notsb[:, :], in1=mxsb[:, :],
                    op=ALU.mult), reads=["not", "mx"], writes=["not"])
        p.op("v", lambda v: v.tensor_copy(ohbf[:, :], ohsb[:, :]),
             reads=["oh"], writes=["ohbf"])
        for t8 in range(8):
            par = t8 % 2
            rowb = 0 if par == 0 else 64
            p.op("t", lambda t, t8=t8, rowb=rowb: t.transpose(
                P7[rowb:rowb + 4, 0:128],
                ohbf[0:128, :].rearrange("p (e t) -> p t e", t=8)[:, t8, :],
                identbf[:, :]),
                reads=["ohbf", "identbf"], writes=["B7", f"B7x{par}"])
            p.op("v", lambda v, t8=t8, rowb=rowb: v.tensor_copy(
                ohfm[0:4, 128 * t8:128 * t8 + 128], P7[rowb:rowb + 4, 0:128]),
                reads=[f"B7x{par}"], writes=["ohfm"])
        for si, (a, b) in enumerate(SUBS3):
            dst = [P4, P5][si]
            dkey = ["B4", "B5"][si]
            w = b - a
            p.op("t", lambda t, a=a, w=w, dst=dst: t.matmul(
                dst[0:32, 0:w], e432bf[:, :], ohfm[0:4, a:a + w],
                start=True, stop=True),
                reads=["ohfm", "e432bf"], writes=[dkey])
            p.op("v", lambda v, a=a, w=w, dst=dst, dkey=dkey: v.tensor_tensor(
                out=zm[:, a:a + w], in0=zsb[:, a:a + w], in1=dst[0:32, 0:w],
                op=ALU.mult),
                reads=[dkey, "zsb"], writes=["zm"])

    for h in range(2):
        base = H3 * h
        for k in range(8):
            p.op("g", lambda g, k=k, base=base: g.dma_start(
                out=olh[:, H3 * k:H3 * (k + 1)],
                in_=olocal[k][:, base:base + H3]),
                writes=[f"AR2:{k}"], is_dma=True)
        for o in range(32):
            wpar = o % 2
            p.op("g", lambda g, o=o, wpar=wpar: g.dma_start(
                out=wt[wpar][:, 0:1024],
                in_=_rear(wo_t, [(128, 128), (16384, 8), (1, 128)],
                          offset=o * 8 * 128 * 128)),
                writes=[f"wt{wpar}"], is_dma=True)
            for si, (a, b) in enumerate(SUBS3):
                gi = 2 * o + si
                par = gi % 2
                w = b - a
                for k in range(8):
                    p.op("t", lambda t, k=k, a=a, w=w, par=par, wpar=wpar: t.matmul(
                        PM[par][:, 0:w], wt[wpar][:, 128 * k:128 * (k + 1)],
                        olh[:, H3 * k + a:H3 * k + a + w],
                        start=(k == 0), stop=(k == 7)),
                        reads=[f"wt{wpar}", f"AR2:{k}"], writes=[f"B{par}"])
                p.op("y", lambda y, o=o, a=a, w=w, par=par, base=base: y.dma_start(
                    out=xres[par][:, 0:w], in_=x1_in[o][:, base + a:base + a + w]),
                    writes=[f"xres{par}"], is_dma=True)
                p.op("v", lambda v, a=a, w=w, par=par: v.tensor_tensor(
                    out=x2f[par][:, 0:w], in0=PM[par][:, 0:w],
                    in1=xres[par][:, 0:w], op=ALU.add),
                    reads=[f"B{par}", f"xres{par}"], writes=[f"x2f{par}"])
                p.op("y", lambda y, o=o, a=a, w=w, par=par, base=base: y.dma_start(
                    out=x2_sp[o][:, base + a:base + a + w], in_=x2f[par][:, 0:w]),
                    reads=[f"x2f{par}"], writes=[f"x2sp:{o}:{h}"], is_dma=True)
                p.op("v", lambda v, o=o, a=a, w=w, par=par: v.tensor_copy(
                    A1(o)[:, a:a + w], x2f[par][:, 0:w]),
                    reads=[f"x2f{par}"], writes=[f"AR1:{o}"])
        stats3(A1, lambda k: f"AR1:{k}")
        for k in range(32):
            p.op("v", lambda v, k=k: v.tensor_sub(
                tmpc[:, :], A1(k), mu128[:, :]),
                reads=[f"AR1:{k}", "mu128"], writes=["xsq0"])
            p.op("v", lambda v, k=k: v.tensor_mul(
                A2(k), tmpc[:, :], r128[:, :]),
                reads=["xsq0", "r128"], writes=[f"AR2:{k}"])
        gate3()
        for o in range(32):
            wpar = o % 2
            p.op("g", lambda g, o=o, wpar=wpar: g.dma_start(
                out=wt[wpar][:, :],
                in_=_rear(wup_t, [(128, 128), (16384, 32), (1, 128)],
                          offset=o * 32 * 128 * 128)),
                writes=[f"wt{wpar}"], is_dma=True)
            for si, (a, b) in enumerate(SUBS3):
                gi = 2 * o + si
                par = gi % 2
                w = b - a
                for k in range(32):
                    p.op("t", lambda t, k=k, a=a, w=w, par=par, wpar=wpar: t.matmul(
                        PM[par][:, 0:w], wt[wpar][:, 128 * k:128 * (k + 1)],
                        A2(k)[:, a:a + w], start=(k == 0), stop=False),
                        reads=[f"wt{wpar}", f"AR2:{k}"], writes=[f"B{par}"])
                p.op("t", lambda t, a=a, w=w, par=par, o=o: t.matmul(
                    PM[par][:, 0:w], lorabbf[:, 128 * o:128 * (o + 1)],
                    zm[:, a:a + w], start=False, stop=True),
                    reads=["lorabbf", "zm"], writes=[f"B{par}"])
                p.op("y", lambda y, o=o, a=a, w=w, par=par, base=base: y.dma_start(
                    out=xres[par][:, 0:w], in_=x2_sp[o][:, base + a:base + a + w]),
                    reads=[f"x2sp:{o}:{h}"], writes=[f"xres{par}"], is_dma=True)
                p.op("v", lambda v, a=a, w=w, par=par: v.scalar_tensor_tensor(
                    out=x2f[par][:, 0:w], in0=PM[par][:, 0:w], scalar=0.5,
                    in1=xres[par][:, 0:w], op0=ALU.mult, op1=ALU.add),
                    reads=[f"B{par}", f"xres{par}"], writes=[f"x2f{par}"])
                p.op("y", lambda y, o=o, a=a, w=w, par=par, base=base: y.dma_start(
                    out=out_fm[o][:, base + a:base + a + w], in_=x2f[par][:, 0:w]),
                    reads=[f"x2f{par}"], writes=["OUT"], is_dma=True)
    p.op("s", lambda s: None, reads=["OUT"], tag="fence")
    p.emit(es)
    return nc, es



def kernel(**inputs):
    ins = inputs
    sh = _prep_shared(ins)

    in_maps1 = [_prep_core_l1(ins, sh, c) for c in range(NCORE)]
    nc1, es1 = build_launch1()
    r1 = run_bass_kernel_spmd(nc1, in_maps1, core_ids=list(range(NCORE)))
    l1 = r1.results
    es1.close()

    in_maps2 = [_prep_core_l2(l1, ins, c) for c in range(NCORE)]
    nc2, es2 = build_launch2()
    r2 = run_bass_kernel_spmd(nc2, in_maps2, core_ids=list(range(NCORE)))
    l2 = r2.results
    es2.close()

    in_maps3 = []
    for c in range(NCORE):
        d = _prep_core_l3(l2, ins, sh, c)
        d["x1_in"] = l1[c]["x1_out"]
        in_maps3.append(d)
    nc3, es3 = build_launch3()
    r3 = run_bass_kernel_spmd(nc3, in_maps3, core_ids=list(range(NCORE)))
    l3 = r3.results
    es3.close()

    out = np.empty((B, S, D), np.float32)
    for c in range(NCORE):
        s, blk = c // 4, c % 4
        t0 = TPC * blk
        of = np.asarray(l3[c]["out_fm"])
        out[s, t0:t0 + TPC] = of.reshape(D, TPC).T
    return out


# version 03 (final)
# speedup vs baseline: 1.2327x; 1.2327x over previous
import numpy as np
import ml_dtypes
import concourse.bass as bass
import concourse.mybir as mybir
from concourse.bass_utils import run_bass_kernel_spmd
from contextlib import ExitStack

F32 = mybir.dt.float32
BF16 = mybir.dt.bfloat16
I32 = mybir.dt.int32
AF = mybir.ActivationFunctionType
ALU = mybir.AluOpType
BF = ml_dtypes.bfloat16

B, S, D = 2, 8192, 4096
NH, NKV, HD = 16, 4, 64
LATENT, KVD = 1024, 256
NCORE = 8
TPC = 2048
HALO = 4
STRIP = HALO + TPC
NE, RANK = 4, 8
EPS = 1e-5

HALF = 1028
SUBS = [(0, 4), (4, 516), (516, 1028)]
REAL_SUBS = [(4, 516), (516, 1028)]



def _ul_q_orig(o, j):
    if o < 4:
        h = 4 * o + j // 32
        d = 2 * (j % 32)
    else:
        h = 4 * (o - 4) + j // 32
        d = 2 * (j % 32) + 1
    return h * 64 + d


def _ul_k_orig(o, j):
    g = j // 32
    d = 2 * (j % 32) + (1 if o == 1 else 0)
    return g * 64 + d


def _prep_shared(ins):
    f = np.float32
    sh = {}
    W = np.asarray(ins["L0_ffn_W_up"], f)
    sh["wup_t"] = np.ascontiguousarray(
        W.reshape(32, 128, 32, 128).transpose(0, 2, 3, 1))
    Wq = np.asarray(ins["L0_attn_w_qkv"], f)
    sh["wqkv_t"] = np.ascontiguousarray(
        Wq.reshape(12, 128, 32, 128).transpose(0, 2, 3, 1))
    Wo = np.asarray(ins["L0_attn_w_o"], f)
    sh["wo_t"] = np.ascontiguousarray(
        Wo.reshape(32, 128, 8, 128).transpose(0, 2, 3, 1))

    A = np.asarray(ins["L0_ffn_lora_A"], f)
    G = np.asarray(ins["L0_ffn_gate_w"], f)
    ag = np.zeros((32, 128, 36), f)
    for e in range(NE):
        for r in range(RANK):
            ag[:, :, 8 * e + r] = A[e, r].reshape(32, 128)
    for e in range(NE):
        ag[:, :, 32 + e] = G[e].reshape(32, 128)
    sh["ag36"] = ag
    Bl = np.asarray(ins["L0_ffn_lora_B"], f)
    lb = np.zeros((32, 32, 128), f)
    for e in range(NE):
        for r in range(RANK):
            lb[:, 8 * e + r, :] = Bl[e, :, r].reshape(32, 128)
    sh["lorab_t"] = lb
    sh["eb4"] = np.asarray(ins["L0_ffn_expert_bias"], f).reshape(4, 1)

    qc1 = np.asarray(ins["L0_attn_qc1_w"], f)
    q1 = np.zeros((8, 3, 128, 128), f)
    for c in range(8):
        for tap in range(3):
            for jj in range(128):
                och = 128 * c + jj
                base = 64 * (jj // 64)
                q1[c, tap, base:base + 64, jj] = qc1[och, :, tap]
    sh["qc1_t"] = q1
    kc1 = np.asarray(ins["L0_attn_kc1_w"], f)
    k1 = np.zeros((2, 3, 128, 128), f)
    for c in range(2):
        for tap in range(3):
            for jj in range(128):
                och = 128 * c + jj
                base = 64 * (jj // 64)
                k1[c, tap, base:base + 64, jj] = kc1[och, :, tap]
    sh["kc1_t"] = k1

    qc2 = np.asarray(ins["L0_attn_qc2_w"], f)
    q2 = np.zeros((8, 8, 3, 128, 128), f)
    ul_rows_q = np.array([[_ul_q_orig(o, j) for j in range(128)]
                          for o in range(8)])
    for o in range(8):
        block = qc2[ul_rows_q[o]]
        q2[o] = block.reshape(128, 8, 128, 3).transpose(1, 3, 2, 0)
    sh["qc2_t"] = np.ascontiguousarray(q2)
    kc2 = np.asarray(ins["L0_attn_kc2_w"], f)
    k2 = np.zeros((2, 2, 3, 128, 128), f)
    ul_rows_k = np.array([[_ul_k_orig(o, j) for j in range(128)]
                          for o in range(2)])
    for o in range(2):
        block = kc2[ul_rows_k[o]]
        k2[o] = block.reshape(128, 2, 128, 3).transpose(1, 3, 2, 0)
    sh["kc2_t"] = np.ascontiguousarray(k2)

    sh["qc1b"] = np.ascontiguousarray(
        np.asarray(ins["L0_attn_qc1_b"], f).reshape(8, 128).T)
    sh["kc1b"] = np.ascontiguousarray(
        np.asarray(ins["L0_attn_kc1_b"], f).reshape(2, 128).T)
    qb2 = np.asarray(ins["L0_attn_qc2_b"], f)
    sh["qc2b"] = np.ascontiguousarray(
        np.stack([qb2[ul_rows_q[o]] for o in range(8)]).T)
    kb2 = np.asarray(ins["L0_attn_kc2_b"], f)
    sh["kc2b"] = np.ascontiguousarray(
        np.stack([kb2[ul_rows_k[o]] for o in range(2)]).T)

    comb = np.zeros((128, 64), f)
    for slot in range(128):
        comb[slot, slot % 64] = 1.0
    sh["combm"] = comb
    eq = np.zeros((2, 64, 128), f)
    ek = np.zeros((2, 64, 128), f)
    for lu in range(2):
        for j in range(128):
            d = 2 * (j % 32) + lu
            eq[lu, d, j] = 0.5 / 4.0
            ek[lu, d, j] = 0.5 / 16.0
    sh["eq_ul"] = eq
    sh["ek_ul"] = ek

    l2q = np.zeros((8, 128, 16), f)
    for u in range(8):
        for slot in range(128):
            l2q[u, slot, 4 * (u % 4) + slot // 32] = 1.0
    sh["l2q"] = l2q
    l2k = np.zeros((128, 4), f)
    for slot in range(128):
        l2k[slot, slot // 32] = 1.0
    sh["l2k"] = l2k
    erq = np.zeros((4, 16, 128), f)
    for u in range(4):
        for j in range(128):
            erq[u, 4 * u + j // 32, j] = 1.0
    sh["erq"] = erq
    erk = np.zeros((4, 128), f)
    for j in range(128):
        erk[j // 32, j] = 1.0
    sh["erk"] = erk

    ident = np.eye(128, dtype=f)
    sh["ident"] = ident
    sh["ones_col"] = np.ones((128, 1), f)
    sh["ones_row"] = np.ones((1, 128), f)
    e432 = np.zeros((4, 32), f)
    for e in range(NE):
        e432[e, 8 * e:8 * e + 8] = 1.0
    sh["exp4to32"] = e432
    sh["kt_rep"] = np.repeat(np.asarray(ins["L0_attn_key_temp"], f).reshape(1, 1),
                             128, axis=0)
    return sh


def _prep_core_l1(ins, sh, c):
    f = np.float32
    s, blk = c // 4, c % 4
    t0 = TPC * blk
    x = np.asarray(ins["L0_attn_x"], f)[s]
    xs = np.zeros((STRIP, D), f)
    lo = max(0, t0 - HALO)
    xs[HALO - (t0 - lo):] = x[lo:t0 + TPC]
    x_fm = np.ascontiguousarray(
        xs.T.reshape(32, 128, STRIP))
    pos = np.arange(t0, t0 + TPC, dtype=np.float64)
    invf = 10000.0 ** (-np.arange(0, HD, 2, dtype=np.float64) / HD)
    slot_f = invf[np.arange(128) % 32]
    ang = slot_f[:, None] * pos[None, :]
    cos_t = np.cos(ang).astype(f)
    sin_t = np.sin(ang).astype(f)
    d = {"x_fm": x_fm, "cos_t": cos_t, "sin_t": sin_t,
         "halo01": np.full((128, 1), 0.0 if t0 == 0 else 1.0, f)}
    for k in ("wup_t", "wqkv_t", "ag36", "lorab_t", "eb4", "qc1_t", "kc1_t",
              "qc2_t", "kc2_t", "qc1b", "kc1b", "qc2b", "kc2b", "combm",
              "eq_ul", "ek_ul", "l2q", "l2k", "erq", "erk", "ident",
              "ones_row", "exp4to32", "kt_rep"):
        d[k] = sh[k]
    return d



class SemBook:

    def __init__(self, nc, es, names):
        self.h = {}
        self.v = {}
        for n in names:
            self.h[n] = es.enter_context(nc.semaphore(n))
            self.v[n] = 0

    def bump(self, n, amt):
        self.v[n] += amt
        return self.h[n], self.v[n]


def _sb(nc, es, name, shape, dtype):
    return es.enter_context(nc.sbuf_tensor(name, shape, dtype))


def _ps(nc, es, name, shape, dtype=F32):
    return es.enter_context(nc.psum_tensor(name, shape, dtype))


def _rear(ap, dims, offset=0):
    return bass.AP(ap.tensor, offset, [list(d) for d in dims])


_DTSZ = {F32: 4, BF16: 2, I32: 4}


class Arena:

    def __init__(self, nc, es, nbytes):
        self.t = es.enter_context(nc.sbuf_tensor("ARENA", [128, nbytes],
                                                 mybir.dt.uint8))
        self.off = 0
        self.nbytes = nbytes

    def reset(self):
        self.off = 0

    def take(self, rows, cols, dtype):
        sz = cols * _DTSZ[dtype]
        self.off = (self.off + 63) & ~63
        assert self.off + sz <= self.nbytes, (self.off, sz, self.nbytes)
        v = self.t[0:rows, self.off:self.off + sz].bitcast(dtype)
        self.off += sz
        return v





class Op:
    __slots__ = ("eng", "fn", "deps", "is_dma", "seq", "need_inc", "semval",
                 "tag", "pool_i", "pool_use")

    def __init__(self, eng, fn, is_dma, tag=""):
        self.eng = eng
        self.fn = fn
        self.deps = []
        self.is_dma = is_dma
        self.seq = 0
        self.need_inc = False
        self.semval = 0
        self.tag = tag


class Prog:

    ENGS = ("t", "v", "s", "g", "y")

    def __init__(self, nc):
        self.nc = nc
        self.ops = {e: [] for e in self.ENGS}
        self.last_writer = {}
        self.readers = {}
        self.all = []
        self.barrier_deps = []

    def barrier(self):
        self.barrier_deps = [self.ops[e][-1] for e in self.ENGS if self.ops[e]]

    def _engine(self, e):
        return {"t": self.nc.tensor, "v": self.nc.vector, "s": self.nc.scalar,
                "g": self.nc.gpsimd, "y": self.nc.sync}[e]

    def op(self, eng, fn, reads=(), writes=(), is_dma=False, tag=""):
        o = Op(eng, fn, is_dma, tag)
        o.seq = len(self.ops[eng])
        deps = list(self.barrier_deps)
        for r in reads:
            w = self.last_writer.get(r)
            if w is not None:
                deps.append(w)
        for r in writes:
            w = self.last_writer.get(r)
            if w is not None:
                deps.append(w)
            deps.extend(self.readers.get(r, ()))
        seen = set()
        for d in deps:
            if d is o or id(d) in seen:
                continue
            seen.add(id(d))
            if d.eng == eng and eng == "t" and not d.is_dma:
                continue
            o.deps.append(d)
        for r in reads:
            self.readers.setdefault(r, []).append(o)
        for r in writes:
            self.last_writer[r] = o
            self.readers[r] = []
        self.ops[eng].append(o)
        self.all.append(o)
        return o

    NPOOL = 6

    def emit(self, es):
        for o in self.all:
            if o.is_dma:
                o.need_inc = True
            for d in o.deps:
                d.need_inc = True
        sems = {}
        pool = {}
        for e in self.ENGS:
            sems[e] = es.enter_context(self.nc.semaphore(f"prog_{e}"))
            pool[e] = [es.enter_context(self.nc.semaphore(f"dma_{e}{i}"))
                       for i in range(self.NPOOL)]
            cnt = 0
            ndma = 0
            uses = [0] * self.NPOOL
            for o in self.ops[e]:
                if o.is_dma:
                    o.pool_i = ndma % self.NPOOL
                    uses[o.pool_i] += 1
                    o.pool_use = uses[o.pool_i]
                    o.semval = 16 * o.pool_use
                    ndma += 1
                else:
                    if o.need_inc:
                        cnt += 1
                    o.semval = cnt
        for e in self.ENGS:
            eng = self._engine(e)
            seen_wm = {}
            for o in self.ops[e]:
                need = {}
                for d in o.deps:
                    if d.is_dma:
                        key = (d.eng, "p", d.pool_i)
                    else:
                        key = (d.eng, "c")
                        if d.semval == 0:
                            continue
                    need[key] = max(need.get(key, 0), d.semval)
                if o.is_dma and o.pool_use > 1:
                    key = (e, "p", o.pool_i)
                    need[key] = max(need.get(key, 0), 16 * (o.pool_use - 1))
                for key, v in need.items():
                    if v > seen_wm.get(key, 0):
                        f = key[0]
                        h = sems[f] if key[1] == "c" else pool[f][key[2]]
                        eng.wait_ge(h, v)
                        seen_wm[key] = v
                ins = o.fn(eng)
                if o.is_dma:
                    ins.then_inc(pool[e][o.pool_i], 16)
                elif o.need_inc:
                    assert ins is not None, f"op {o.tag} needs inc, fn gave None"
                    ins.then_inc(sems[e], 1)
        return sems, pool



L1_DEBUG = False


def build_launch1():
    nc = bass.Bass("TRN2", target_bir_lowering=False)
    es = ExitStack()
    p = Prog(nc)

    def din(name, shape, dt=F32):
        return nc.dram_tensor(name, shape, dt, kind="ExternalInput").ap()

    x_fm = din("x_fm", [32, 128, STRIP])
    wup_t = din("wup_t", [32, 32, 128, 128])
    wqkv_t = din("wqkv_t", [12, 32, 128, 128])
    ag36 = din("ag36", [32, 128, 36])
    lorab_t = din("lorab_t", [32, 32, 128])
    eb4 = din("eb4", [4, 1])
    qc1_t = din("qc1_t", [8, 3, 128, 128])
    kc1_t = din("kc1_t", [2, 3, 128, 128])
    qc2_t = din("qc2_t", [8, 8, 3, 128, 128])
    kc2_t = din("kc2_t", [2, 2, 3, 128, 128])
    qc1b = din("qc1b", [128, 8])
    kc1b = din("kc1b", [128, 2])
    qc2b = din("qc2b", [128, 8])
    kc2b = din("kc2b", [128, 2])
    combm = din("combm", [128, 64])
    eq_ul = din("eq_ul", [2, 64, 128])
    ek_ul = din("ek_ul", [2, 64, 128])
    l2q = din("l2q", [8, 128, 16])
    l2k = din("l2k", [128, 4])
    erq = din("erq", [4, 16, 128])
    erk = din("erk", [4, 128])
    ident = din("ident", [128, 128])
    ones_row = din("ones_row", [1, 128])
    exp4to32 = din("exp4to32", [4, 32])
    cos_in = din("cos_t", [128, TPC])
    sin_in = din("sin_t", [128, TPC])
    kt_rep = din("kt_rep", [128, 1])
    halo01 = din("halo01", [128, 1])

    x1_out = nc.dram_tensor("x1_out", [32, 128, TPC], F32, kind="ExternalOutput").ap()
    qblob = nc.dram_tensor("qblob", [8, 128, TPC], BF16, kind="ExternalOutput").ap()
    kblob = nc.dram_tensor("kblob", [2, 128, TPC], BF16, kind="ExternalOutput").ap()
    vblob = nc.dram_tensor("vblob", [TPC, 256], BF16, kind="ExternalOutput").ap()
    qkv_sp = nc.dram_tensor("qkv_sp", [12, 128, STRIP], BF16, kind="Internal").ap()
    if L1_DEBUG:
        dbg_mu = nc.dram_tensor("dbg_mu", [1, HALF], BF16, kind="ExternalOutput").ap()
        dbg_r = nc.dram_tensor("dbg_r", [1, HALF], BF16, kind="ExternalOutput").ap()
        dbg_lnx = nc.dram_tensor("dbg_lnx", [32, 128, HALF], BF16, kind="ExternalOutput").ap()
        dbg_zm = nc.dram_tensor("dbg_zm", [32, HALF], BF16, kind="ExternalOutput").ap()
        dbg_lsb = nc.dram_tensor("dbg_lsb", [4, HALF], F32, kind="ExternalOutput").ap()
        dbg_tm = nc.dram_tensor("dbg_tm", [128, 36], F32, kind="ExternalOutput").ap()
        dbg_oh = nc.dram_tensor("dbg_oh", [128, 36], F32, kind="ExternalOutput").ap()
        dbg_ohfm = nc.dram_tensor("dbg_ohfm", [4, 1040], BF16, kind="ExternalOutput").ap()

    eb4sb = _sb(nc, es, "eb4sb", [4, 1], F32)
    onesrbf = _sb(nc, es, "onesrbf", [1, 128], BF16)
    identbf = _sb(nc, es, "identbf", [128, 128], BF16)
    e432bf = _sb(nc, es, "e432bf", [4, 32], BF16)
    halosb = _sb(nc, es, "halosb", [128, 1], F32)

    P0 = _ps(nc, es, "P0", [128, 512])
    P1 = _ps(nc, es, "P1", [128, 512])
    P2 = _ps(nc, es, "B2", [128, 512])
    P3 = _ps(nc, es, "B3", [128, 512])
    P4 = _ps(nc, es, "B4", [128, 512])
    P5 = _ps(nc, es, "B5", [128, 512])
    P6 = _ps(nc, es, "B6", [64, 512])
    P7 = _ps(nc, es, "B7", [128, 128], BF16)
    PM = [P0, P1]

    p.op("g", lambda g: g.dma_start(out=ag36bf[:, :],
         in_=_rear(ag36, [(36, 128), (4608, 32), (1, 36)])),
         writes=["ag36bf"], is_dma=True)
    p.op("g", lambda g: g.dma_start(out=lorabbf[:, :],
         in_=_rear(lorab_t, [(128, 32), (4096, 32), (1, 128)])),
         writes=["lorabbf"], is_dma=True)
    p.op("g", lambda g: g.dma_start(out=onesrbf[:, :], in_=ones_row),
         writes=["onesrbf"], is_dma=True)
    p.op("g", lambda g: g.dma_start(out=identbf[:, :], in_=ident),
         writes=["identbf"], is_dma=True)
    p.op("g", lambda g: g.dma_start(out=e432bf[:, :], in_=exp4to32),
         writes=["e432bf"], is_dma=True)
    p.op("y", lambda y: y.dma_start(out=eb4sb[:, :], in_=eb4),
         writes=["eb4sb"], is_dma=True)
    p.op("y", lambda y: y.dma_start(out=halosb[:, :], in_=halo01),
         writes=["halosb"], is_dma=True)

    ar = Arena(nc, es, 206000)
    AR1 = ar.take(128, 32 * HALF, BF16)
    AR2 = ar.take(128, 32 * HALF, BF16)
    ag36bf = ar.take(128, 32 * 36, BF16)
    lorabbf = ar.take(32, 32 * 128, BF16)
    wt = [ar.take(128, 4096, BF16) for i in range(2)]
    xsq = [ar.take(128, HALF, BF16) for i in range(2)]
    tmpc = xsq[0]
    mu128 = ar.take(128, HALF, BF16)
    r128 = ar.take(128, HALF, BF16)
    rowA = ar.take(1, HALF, F32)
    rowB = ar.take(1, HALF, F32)
    rowC = None
    mubf = ar.take(1, HALF, BF16)
    rbf = ar.take(1, HALF, BF16)
    zsb = ar.take(32, HALF, F32)
    zm = ar.take(32, HALF, BF16)
    lsb = ar.take(4, HALF, F32)
    rowC = lsb[0:1, 0:HALF]
    lsbbf = ar.take(4, HALF, BF16)
    tmsb = ar.take(128, 36, F32)
    mxsb = ar.take(128, 9, F32)
    gesb = ar.take(128, 36, F32)
    ohsb = ar.take(128, 36, F32)
    notsb = ar.take(128, 9, F32)
    ohbf = ar.take(128, 36, BF16)
    ohfm = ar.take(4, 1040, BF16)
    x1f = [ar.take(128, 512, F32) for i in range(2)]
    qev = [xsq[i][:, 0:512] for i in range(2)]
    xres = [ar.take(128, 512, F32) for i in range(2)]

    def A1(k):
        return AR1[:, HALF * k:HALF * (k + 1)]

    def A2(k):
        return AR2[:, HALF * k:HALF * (k + 1)]

    def stats_rows_pass(src_of_k, key_of_k):
        PSX = [P0, P1, P2]
        PSQ = [P3, P4, P5]
        for k in range(32):
            par = k % 2
            p.op("s", lambda s, k=k, par=par: s.activation(
                out=xsq[par][:, :], in_=src_of_k(k), func=AF.Square),
                reads=[key_of_k(k)], writes=[f"xsq{par}"])
            for si, (a, b) in enumerate(SUBS):
                w = b - a
                p.op("t", lambda t, k=k, a=a, w=w, si=si: t.matmul(
                    PSX[si][0:1, 0:w], onescbf_v[:, :], src_of_k(k)[:, a:a + w],
                    start=(k == 0), stop=(k == 31)),
                    reads=[key_of_k(k), "onescbf"], writes=[("B0","B1","B2")[si]])
                p.op("t", lambda t, k=k, a=a, w=w, par=par, si=si: t.matmul(
                    PSQ[si][0:1, 0:w], onescbf_v[:, :], xsq[par][:, a:a + w],
                    start=(k == 0), stop=(k == 31)),
                    reads=[f"xsq{par}", "onescbf"], writes=[("B3","B4","B5")[si]])
        def ev_rows(v):
            v.tensor_copy(rowA[0:1, 0:4], PSX[0][0:1, 0:4])
            v.tensor_copy(rowA[0:1, 4:516], PSX[1][0:1, 0:512])
            v.tensor_copy(rowA[0:1, 516:1028], PSX[2][0:1, 0:512])
            v.tensor_copy(rowB[0:1, 0:4], PSQ[0][0:1, 0:4])
            v.tensor_copy(rowB[0:1, 4:516], PSQ[1][0:1, 0:512])
            return v.tensor_copy(rowB[0:1, 516:1028], PSQ[2][0:1, 0:512])
        p.op("v", ev_rows,
             reads=["B0", "B1", "B2", "B3", "B4", "B5"],
             writes=["rowA", "rowB"])
        p.op("v", lambda v: v.tensor_scalar(
            out=rowA[:, :], in0=rowA[:, :], scalar1=1.0 / 4096,
            scalar2=None, op0=ALU.mult), reads=["rowA"], writes=["rowA"])
        p.op("v", lambda v: v.tensor_scalar(
            out=rowB[:, :], in0=rowB[:, :], scalar1=1.0 / 4096,
            scalar2=None, op0=ALU.mult), reads=["rowB"], writes=["rowB"])
        p.op("v", lambda v: v.tensor_tensor(
            out=rowC[:, :], in0=rowA[:, :], in1=rowA[:, :], op=ALU.mult),
            reads=["rowA"], writes=["lsb"])
        p.op("v", lambda v: v.tensor_tensor(
            out=rowB[:, :], in0=rowB[:, :], in1=rowC[:, :], op=ALU.subtract),
            reads=["rowB", "lsb"], writes=["rowB"])
        p.op("v", lambda v: v.tensor_scalar(
            out=rowB[:, :], in0=rowB[:, :], scalar1=EPS, scalar2=None,
            op0=ALU.add), reads=["rowB"], writes=["rowB"])
        p.op("s", lambda s: s.activation(out=rowC[:, :], in_=rowB[:, :],
                                         func=AF.Ln),
             reads=["rowB"], writes=["lsb"])
        p.op("s", lambda s: s.activation(out=rowC[:, :], in_=rowC[:, :],
                                         func=AF.Exp, scale=-0.5),
             reads=["lsb"], writes=["lsb"])
        p.op("v", lambda v: v.tensor_copy(mubf[:, :], rowA[:, :]),
             reads=["rowA"], writes=["mubf"])
        p.op("v", lambda v: v.tensor_copy(rbf[:, :], rowC[:, :]),
             reads=["lsb"], writes=["rbf"])
        for si, (a, b) in enumerate(SUBS):
            w = b - a
            p.op("t", lambda t, a=a, w=w: t.matmul(
                P4[:, 0:w], onesrbf[0:1, :], mubf[0:1, a:a + w],
                start=True, stop=True),
                reads=["mubf", "onesrbf"], writes=["B4"])
            p.op("t", lambda t, a=a, w=w: t.matmul(
                P5[:, 0:w], onesrbf[0:1, :], rbf[0:1, a:a + w],
                start=True, stop=True),
                reads=["rbf", "onesrbf"], writes=["B5"])
            p.op("v", lambda v, a=a, w=w: v.tensor_copy(
                mu128[:, a:a + w], P4[:, 0:w]), reads=["B4"], writes=["mu128"])
            p.op("v", lambda v, a=a, w=w: v.tensor_copy(
                r128[:, a:a + w], P5[:, 0:w]), reads=["B5"], writes=["r128"])

    def ln_build(src_of_k, skey, dst_of_k, dkey):
        for k in range(32):
            p.op("v", lambda v, k=k: v.tensor_sub(
                tmpc[:, :], src_of_k(k), mu128[:, :]),
                reads=[skey(k), "mu128"], writes=["xsq0"])
            p.op("v", lambda v, k=k: v.tensor_mul(
                dst_of_k(k), tmpc[:, :], r128[:, :]),
                reads=["xsq0", "r128"], writes=[dkey(k)])

    def gate_pass(lnx_of_k, lnkey):
        for k in range(32):
            for si, (a, b) in enumerate(SUBS):
                dst = [P6, P4, P5][si]
                dkey = ["B6", "B4", "B5"][si]
                w = b - a
                p.op("t", lambda t, k=k, a=a, w=w, dst=dst: t.matmul(
                    dst[0:36, 0:w], ag36bf[:, 36 * k:36 * (k + 1)],
                    lnx_of_k(k)[:, a:a + w], start=(k == 0), stop=(k == 31)),
                    reads=[lnkey(k), "ag36bf"], writes=[dkey])
        def ev_z(v):
            v.tensor_copy(zsb[:, 4:516], P4[0:32, 0:512])
            v.tensor_copy(zsb[:, 516:1028], P5[0:32, 0:512])
            v.tensor_copy(zsb[:, 0:4], P6[0:32, 0:4])
            v.tensor_copy(lsb[:, 4:516], P4[32:36, 0:512])
            v.tensor_copy(lsb[:, 516:1028], P5[32:36, 0:512])
            return v.tensor_copy(lsb[:, 0:4], P6[32:36, 0:4])
        p.op("v", ev_z, reads=["B4", "B5", "B6"], writes=["zsb", "lsb"])
        p.op("v", lambda v: v.tensor_scalar(
            out=lsb[:, :], in0=lsb[:, :], scalar1=eb4sb[:, :],
            scalar2=None, op0=ALU.add), reads=["lsb", "eb4sb"], writes=["lsb"])
        p.op("v", lambda v: v.tensor_copy(lsbbf[:, :], lsb[:, :]),
             reads=["lsb"], writes=["lsbbf"])
        for t8 in range(9):
            w = 128 if t8 < 8 else 4
            p.op("t", lambda t, t8=t8, w=w: t.transpose(
                P7[0:w, 4 * t8:4 * t8 + 4], lsbbf[0:4, 128 * t8:128 * t8 + w],
                identbf[0:4, 0:4]),
                reads=["lsbbf", "identbf"], writes=["B7"])
        def ev_tm(v):
            tm3 = P7[:, 0:36].rearrange("p (t f) -> p t f", f=4)
            for e in range(4):
                v.tensor_copy(tmsb[:, 9 * e:9 * e + 9], tm3[:, :, e])
            return v.tensor_copy(mxsb[:, :], tm3[:, :, 0])
        p.op("v", ev_tm, reads=["B7"], writes=["tm", "mx"])
        p.op("v", lambda v: v.tensor_tensor(
            out=notsb[:, :], in0=tmsb[:, 0:9], in1=tmsb[:, 9:18], op=ALU.max),
            reads=["tm"], writes=["not"])
        p.op("v", lambda v: v.tensor_tensor(
            out=mxsb[:, :], in0=tmsb[:, 18:27], in1=tmsb[:, 27:36], op=ALU.max),
            reads=["tm"], writes=["mx"])
        p.op("v", lambda v: v.tensor_tensor(
            out=mxsb[:, :], in0=mxsb[:, :], in1=notsb[:, :], op=ALU.max),
            reads=["mx", "not"], writes=["mx"])
        for e in range(4):
            p.op("v", lambda v, e=e: v.tensor_tensor(
                out=gesb[:, 9 * e:9 * e + 9], in0=tmsb[:, 9 * e:9 * e + 9],
                in1=mxsb[:, :], op=ALU.is_ge),
                reads=["tm", "mx"], writes=["ge"])
        p.op("v", lambda v: v.tensor_copy(ohsb[:, 0:9], gesb[:, 0:9]),
             reads=["ge"], writes=["oh"])
        p.op("v", lambda v: v.tensor_scalar(
            out=notsb[:, :], in0=gesb[:, 0:9], scalar1=-1.0, scalar2=1.0,
            op0=ALU.mult, op1=ALU.add), reads=["ge"], writes=["not"])
        for e in range(1, 4):
            p.op("v", lambda v, e=e: v.tensor_tensor(
                out=ohsb[:, 9 * e:9 * e + 9], in0=gesb[:, 9 * e:9 * e + 9],
                in1=notsb[:, :], op=ALU.mult),
                reads=["ge", "not"], writes=["oh"])
            if e < 3:
                p.op("v", lambda v, e=e: v.tensor_scalar(
                    out=mxsb[:, :], in0=gesb[:, 9 * e:9 * e + 9],
                    scalar1=-1.0, scalar2=1.0, op0=ALU.mult, op1=ALU.add),
                    reads=["ge"], writes=["mx"])
                p.op("v", lambda v: v.tensor_tensor(
                    out=notsb[:, :], in0=notsb[:, :], in1=mxsb[:, :],
                    op=ALU.mult), reads=["not", "mx"], writes=["not"])
        p.op("v", lambda v: v.tensor_copy(ohbf[:, :], ohsb[:, :]),
             reads=["oh"], writes=["ohbf"])
        for t8 in range(9):
            w = 128 if t8 < 8 else 4
            par = t8 % 2
            rowb = 0 if par == 0 else 64
            p.op("t", lambda t, t8=t8, w=w, rowb=rowb: t.transpose(
                P7[rowb:rowb + 4, 0:w],
                ohbf[0:w, :].rearrange("p (e t) -> p t e", t=9)[:, t8, :],
                identbf[0:w, 0:w]),
                reads=["ohbf", "identbf"], writes=["B7", f"B7x{par}"])
            p.op("v", lambda v, t8=t8, w=w, rowb=rowb: v.tensor_copy(
                ohfm[0:4, 128 * t8:128 * t8 + w], P7[rowb:rowb + 4, 0:w]),
                reads=[f"B7x{par}"], writes=["ohfm"])
        for si, (a, b) in enumerate(SUBS):
            dst = [P6, P4, P5][si]
            dkey = ["B6", "B4", "B5"][si]
            w = b - a
            p.op("t", lambda t, a=a, w=w, dst=dst: t.matmul(
                dst[0:32, 0:w], e432bf[:, :], ohfm[0:4, a:a + w],
                start=True, stop=True),
                reads=["ohfm", "e432bf"], writes=[dkey])
            p.op("v", lambda v, a=a, w=w, dst=dst: v.tensor_tensor(
                out=zm[:, a:a + w], in0=zsb[:, a:a + w], in1=dst[0:32, 0:w],
                op=ALU.mult),
                reads=[dkey, "zsb"], writes=["zm"])

    onescbf_v = _sb(nc, es, "onescbf", [128, 1], BF16)
    p.op("v", lambda v: v.memset(onescbf_v[:, :], 1.0), writes=["onescbf"])

    for h in range(2):
        base = 1024 * h
        for k in range(32):
            p.op("g", lambda g, k=k, base=base: g.dma_start(
                out=A1(k), in_=x_fm[k][:, base:base + HALF]),
                writes=[f"AR1:{k}"], is_dma=True, tag=f"xld{h}:{k}")
        stats_rows_pass(A1, lambda k: f"AR1:{k}")
        ln_build(A1, lambda k: f"AR1:{k}", A2, lambda k: f"AR2:{k}")
        gate_pass(A2, lambda k: f"AR2:{k}")
        if L1_DEBUG and h == 0:
            p.op("y", lambda y: y.dma_start(out=dbg_lsb, in_=lsb[:, :]),
                 reads=["lsb"], writes=["OUT"], is_dma=True)
            p.op("y", lambda y: y.dma_start(out=dbg_tm, in_=tmsb[:, :]),
                 reads=["tm"], writes=["OUT"], is_dma=True)
            p.op("y", lambda y: y.dma_start(out=dbg_oh, in_=ohsb[:, :]),
                 reads=["oh"], writes=["OUT"], is_dma=True)
            p.op("y", lambda y: y.dma_start(out=dbg_ohfm, in_=ohfm[:, :]),
                 reads=["ohfm"], writes=["OUT"], is_dma=True)
            p.op("y", lambda y: y.dma_start(out=dbg_mu, in_=mubf[:, :]),
                 reads=["mubf"], writes=["OUT"], is_dma=True)
            p.op("y", lambda y: y.dma_start(out=dbg_r, in_=rbf[:, :]),
                 reads=["rbf"], writes=["OUT"], is_dma=True)
            for k in range(32):
                p.op("y", lambda y, k=k: y.dma_start(out=dbg_lnx[k], in_=A2(k)),
                     reads=[f"AR2:{k}"], writes=["OUT"], is_dma=True)
            p.op("y", lambda y: y.dma_start(out=dbg_zm, in_=zm[:, :]),
                 reads=["zm"], writes=["OUT"], is_dma=True)
        for o in range(32):
            wpar = o % 2
            p.op("g", lambda g, o=o, wpar=wpar: g.dma_start(
                out=wt[wpar][:, :],
                in_=_rear(wup_t, [(128, 128), (16384, 32), (1, 128)],
                          offset=o * 32 * 128 * 128)),
                writes=[f"wt{wpar}"], is_dma=True, tag=f"wup{h}:{o}")
            for si, (a, b) in enumerate(SUBS):
                gi = 3 * o + si
                par = gi % 2
                w = b - a
                for k in range(32):
                    p.op("t", lambda t, k=k, a=a, w=w, par=par, wpar=wpar: t.matmul(
                        PM[par][:, 0:w], wt[wpar][:, 128 * k:128 * (k + 1)],
                        A2(k)[:, a:a + w], start=(k == 0), stop=False),
                        reads=[f"wt{wpar}", f"AR2:{k}"], writes=[f"B{par}"])
                p.op("t", lambda t, a=a, w=w, par=par, o=o: t.matmul(
                    PM[par][:, 0:w], lorabbf[:, 128 * o:128 * (o + 1)],
                    zm[:, a:a + w], start=False, stop=True),
                    reads=["lorabbf", "zm"], writes=[f"B{par}"])
                p.op("y", lambda y, o=o, a=a, w=w, par=par, base=base: y.dma_start(
                    out=xres[par][:, 0:w], in_=x_fm[o][:, base + a:base + a + w]),
                    writes=[f"xres{par}"], is_dma=True)
                p.op("v", lambda v, o=o, a=a, w=w, par=par: v.scalar_tensor_tensor(
                    out=x1f[par][:, 0:w], in0=PM[par][:, 0:w], scalar=0.5,
                    in1=xres[par][:, 0:w], op0=ALU.mult, op1=ALU.add),
                    reads=[f"B{par}", f"xres{par}"], writes=[f"x1f{par}"])
                p.op("v", lambda v, o=o, a=a, w=w, par=par: v.tensor_copy(
                    A1(o)[:, a:a + w], x1f[par][:, 0:w]),
                    reads=[f"x1f{par}"], writes=[f"AR1:{o}"])
                if si > 0:
                    p.op("y", lambda y, o=o, a=a, w=w, par=par, base=base: y.dma_start(
                        out=x1_out[o][:, base + a - HALO:base + a - HALO + w],
                        in_=x1f[par][:, 0:w]),
                        reads=[f"x1f{par}"], writes=["OUT"], is_dma=True)
        stats_rows_pass(A1, lambda k: f"AR1:{k}")
        ln_build(A1, lambda k: f"AR1:{k}", A2, lambda k: f"AR2:{k}")
        for o in range(12):
            wpar = o % 2
            p.op("g", lambda g, o=o, wpar=wpar: g.dma_start(
                out=wt[wpar][:, :],
                in_=_rear(wqkv_t, [(128, 128), (16384, 32), (1, 128)],
                          offset=o * 32 * 128 * 128)),
                writes=[f"wt{wpar}"], is_dma=True, tag=f"wqkv{h}:{o}")
            for si, (a, b) in enumerate(SUBS):
                gi = 3 * o + si
                par = gi % 2
                w = b - a
                for k in range(32):
                    p.op("t", lambda t, k=k, a=a, w=w, par=par, wpar=wpar: t.matmul(
                        PM[par][:, 0:w], wt[wpar][:, 128 * k:128 * (k + 1)],
                        A2(k)[:, a:a + w], start=(k == 0), stop=(k == 31)),
                        reads=[f"wt{wpar}", f"AR2:{k}"], writes=[f"B{par}"])
                p.op("v", lambda v, o=o, a=a, w=w, par=par: v.tensor_copy(
                    qev[par][:, 0:w], PM[par][:, 0:w]),
                    reads=[f"B{par}"], writes=[f"xsq{par}"])
                p.op("y", lambda y, o=o, a=a, w=w, par=par, base=base: y.dma_start(
                    out=qkv_sp[o][:, base + a:base + a + w],
                    in_=qev[par][:, 0:w]),
                    reads=[f"xsq{par}"], writes=[f"qsp:{o}:{h}"], is_dma=True)

    _l1_era_c(nc, p, ar, locals())
    p.op("s", lambda s: None, reads=["OUT"], tag="fence")
    p.emit(es)
    return nc, es


def _l1_era_c(nc, p, ar, env):
    g = env
    qkv_sp, qblob, kblob, vblob = g["qkv_sp"], g["qblob"], g["kblob"], g["vblob"]
    qc1_t, kc1_t, qc2_t, kc2_t = g["qc1_t"], g["kc1_t"], g["qc2_t"], g["kc2_t"]
    qc1b, kc1b, qc2b, kc2b = g["qc1b"], g["kc1b"], g["qc2b"], g["kc2b"]
    combm, eq_ul, ek_ul = g["combm"], g["eq_ul"], g["ek_ul"]
    l2q, l2k, erq, erk = g["l2q"], g["l2k"], g["erq"], g["erk"]
    cos_in, sin_in, kt_rep = g["cos_in"], g["sin_in"], g["kt_rep"]
    identbf, halosb = g["identbf"], g["halosb"]
    P0, P1, P2, P3 = g["P0"], g["P1"], g["P2"], g["P3"]
    P4, P5, P7 = g["P4"], g["P5"], g["P7"]
    PM = [P0, P1]

    p.barrier()
    ar.reset()
    c1q = ar.take(128, 8 * STRIP, BF16)
    c1k = ar.take(128, 2 * STRIP, BF16)
    c2q = ar.take(128, 8 * TPC, BF16)
    c2k = ar.take(128, 2 * TPC, BF16)
    qst = [ar.take(128, STRIP, BF16) for _ in range(2)]
    qmsb = ar.take(64, TPC, BF16)
    kmsb = ar.take(64, TPC, BF16)
    rq_f = ar.take(16, TPC, F32)
    rqbf = ar.take(16, TPC, BF16)
    rk_f = ar.take(4, TPC, F32)
    rkbf = ar.take(4, TPC, BF16)
    cosT = ar.take(128, TPC, F32)
    sinT = ar.take(128, TPC, F32)
    ce = ar.take(128, 512, F32)
    se_ = ar.take(128, 512, F32)
    m1 = ar.take(128, 512, F32)
    m2 = ar.take(128, 512, F32)
    qup = ar.take(128, 512, BF16)
    qlo = ar.take(128, 512, BF16)
    vstg = [ar.take(128, 128, BF16) for _ in range(2)]
    cw = [ar.take(128, 3072, BF16) for _ in range(2)]
    qc1bf = ar.take(128, 3072, BF16)
    kc1bf = ar.take(128, 768, BF16)
    kc2bf = ar.take(128, 1536, BF16)
    combbf = ar.take(128, 64, BF16)
    eqbf = ar.take(64, 256, BF16)
    ekbf = ar.take(64, 256, BF16)
    l2qbf = ar.take(128, 128, BF16)
    l2kbf = ar.take(128, 4, BF16)
    erqbf = ar.take(16, 512, BF16)
    erkbf = ar.take(4, 128, BF16)
    nsq = ar.take(16, 512, F32)
    sqv = [ar.take(128, 512, BF16) for _ in range(2)]
    kt_sb = ar.take(128, 1, F32)
    b1q = ar.take(128, 8, F32)
    b1k = ar.take(128, 2, F32)
    b2q = ar.take(128, 8, F32)
    b2k = ar.take(128, 2, F32)

    p.op("g", lambda gg: gg.dma_start(out=qc1bf[:, :],
         in_=_rear(qc1_t, [(128, 128), (49152, 8), (16384, 3), (1, 128)])),
         writes=["qc1bf"], is_dma=True)
    p.op("g", lambda gg: gg.dma_start(out=kc1bf[:, :],
         in_=_rear(kc1_t, [(128, 128), (49152, 2), (16384, 3), (1, 128)])),
         writes=["kc1bf"], is_dma=True)
    p.op("g", lambda gg: gg.dma_start(out=kc2bf[:, :],
         in_=_rear(kc2_t, [(128, 128), (49152, 4), (16384, 3), (1, 128)])),
         writes=["kc2bf"], is_dma=True)
    p.op("g", lambda gg: gg.dma_start(out=combbf[:, :], in_=combm),
         writes=["combbf"], is_dma=True)
    p.op("g", lambda gg: gg.dma_start(out=eqbf[:, :],
         in_=_rear(eq_ul, [(128, 64), (8192, 2), (1, 128)])),
         writes=["eqbf"], is_dma=True)
    p.op("g", lambda gg: gg.dma_start(out=ekbf[:, :],
         in_=_rear(ek_ul, [(128, 64), (8192, 2), (1, 128)])),
         writes=["ekbf"], is_dma=True)
    p.op("g", lambda gg: gg.dma_start(out=l2qbf[:, :],
         in_=_rear(l2q, [(16, 128), (2048, 8), (1, 16)])),
         writes=["l2qbf"], is_dma=True)
    p.op("g", lambda gg: gg.dma_start(out=l2kbf[:, :], in_=l2k),
         writes=["l2kbf"], is_dma=True)
    p.op("g", lambda gg: gg.dma_start(out=erqbf[:, :],
         in_=_rear(erq, [(128, 16), (2048, 4), (1, 128)])),
         writes=["erqbf"], is_dma=True)
    p.op("g", lambda gg: gg.dma_start(out=erkbf[:, :], in_=erk),
         writes=["erkbf"], is_dma=True)
    p.op("y", lambda y: y.dma_start(out=cosT[:, :], in_=cos_in),
         writes=["cosT"], is_dma=True)
    p.op("y", lambda y: y.dma_start(out=sinT[:, :], in_=sin_in),
         writes=["sinT"], is_dma=True)
    p.op("y", lambda y: y.dma_start(out=b1q[:, :], in_=qc1b),
         writes=["b1q"], is_dma=True)
    p.op("y", lambda y: y.dma_start(out=b1k[:, :], in_=kc1b),
         writes=["b1k"], is_dma=True)
    p.op("y", lambda y: y.dma_start(out=b2q[:, :], in_=qc2b),
         writes=["b2q"], is_dma=True)
    p.op("y", lambda y: y.dma_start(out=b2k[:, :], in_=kc2b),
         writes=["b2k"], is_dma=True)
    p.op("y", lambda y: y.dma_start(out=kt_sb[:, :], in_=kt_rep),
         writes=["kt_sb"], is_dma=True)

    C1SUBS = [(2, 4), (4, 516), (516, 1028), (1028, 1540), (1540, 2052)]
    C2SUBS = [(4, 516), (516, 1028), (1028, 1540), (1540, 2052)]

    def qkv_stream(c, par):
        p.op("y", lambda y, c=c, par=par: y.dma_start(
            out=qst[par][:, :], in_=qkv_sp[c]),
            reads=[f"qsp:{c}:0", f"qsp:{c}:1"], writes=[f"qst{par}"],
            is_dma=True)

    def conv1(nchunk, wsrc, wkey, bias, bkey, dst, dstkey, gi0):
        for c in range(nchunk):
            par = c % 2
            qkv_stream(c if dstkey == "c1q" else 8 + c, par)
            for si, (a, b) in enumerate(C1SUBS):
                w = b - a
                pp = (gi0 + c * len(C1SUBS) + si) % 2
                for tap in range(3):
                    p.op("t", lambda t, c=c, tap=tap, a=a, w=w, par=par, pp=pp: t.matmul(
                        PM[pp][:, 0:w],
                        wsrc[:, (c * 3 + tap) * 128:(c * 3 + tap) * 128 + 128],
                        qst[par][:, a - 2 + tap:a - 2 + tap + w],
                        start=(tap == 0), stop=(tap == 2)),
                        reads=[wkey, f"qst{par}"], writes=[f"B{pp}"])
                def ev(v, c=c, a=a, w=w, pp=pp):
                    return v.tensor_scalar(
                        out=dst[:, c * STRIP + a:c * STRIP + a + w],
                        in0=PM[pp][:, 0:w], scalar1=bias[:, c:c + 1],
                        scalar2=None, op0=ALU.add)
                p.op("v", ev, reads=[f"B{pp}", bkey], writes=[dstkey])
            def hz(v, c=c):
                return v.tensor_scalar(
                    out=dst[:, c * STRIP + 2:c * STRIP + 4],
                    in0=dst[:, c * STRIP + 2:c * STRIP + 4],
                    scalar1=halosb[:, :], scalar2=None, op0=ALU.mult)
            p.op("v", hz, reads=["halosb", dstkey], writes=[dstkey])
        return gi0 + nchunk * len(C1SUBS)

    gi = 0
    gi = conv1(8, qc1bf, "qc1bf", b1q, "b1q", c1q, "c1q", gi)
    gi = conv1(2, kc1bf, "kc1bf", b1k, "b1k", c1k, "c1k", gi)

    for si, (a, b) in enumerate(C2SUBS):
        w = b - a
        for c in range(8):
            par = c % 2
            qkv_stream(c, par)
            p.op("t", lambda t, c=c, a=a, w=w, par=par: t.matmul(
                P2[0:64, 0:w], combbf[:, :], qst[par][:, a:a + w],
                start=(c == 0), stop=(c == 7)),
                reads=["combbf", f"qst{par}"], writes=["B2"])
        p.op("v", lambda v, a=a, w=w: v.tensor_copy(
            qmsb[:, a - 4:a - 4 + w], P2[0:64, 0:w]),
            reads=["B2"], writes=["qmsb"])
        for c in range(2):
            par = c % 2
            qkv_stream(8 + c, par)
            p.op("t", lambda t, c=c, a=a, w=w, par=par: t.matmul(
                P3[0:64, 0:w], combbf[:, :], qst[par][:, a:a + w],
                start=(c == 0), stop=(c == 1)),
                reads=["combbf", f"qst{par}"], writes=["B3"])
        p.op("v", lambda v, a=a, w=w: v.tensor_copy(
            kmsb[:, a - 4:a - 4 + w], P3[0:64, 0:w]),
            reads=["B3"], writes=["kmsb"])

    for u in range(8):
        wpar = u % 2
        p.op("g", lambda gg, u=u, wpar=wpar: gg.dma_start(
            out=cw[wpar][:, :],
            in_=_rear(qc2_t, [(128, 128), (49152, 8), (16384, 3), (1, 128)],
                      offset=u * 8 * 3 * 128 * 128)),
            writes=[f"cw{wpar}"], is_dma=True)
        for si, (a, b) in enumerate(C2SUBS):
            w = b - a
            pp = (u * len(C2SUBS) + si) % 2
            for k in range(8):
                for tap in range(3):
                    p.op("t", lambda t, u=u, k=k, tap=tap, a=a, w=w, pp=pp, wpar=wpar: t.matmul(
                        PM[pp][:, 0:w],
                        cw[wpar][:, (k * 3 + tap) * 128:(k * 3 + tap) * 128 + 128],
                        c1q[:, k * STRIP + a - 2 + tap:k * STRIP + a - 2 + tap + w],
                        start=(k == 0 and tap == 0), stop=False),
                        reads=[f"cw{wpar}", "c1q"], writes=[f"B{pp}"])
            lu = 0 if u < 4 else 1
            p.op("t", lambda t, u=u, a=a, w=w, pp=pp, lu=lu: t.matmul(
                PM[pp][:, 0:w], eqbf[:, lu * 128:lu * 128 + 128],
                kmsb[:, a - 4:a - 4 + w], start=False, stop=True),
                reads=["eqbf", "kmsb"], writes=[f"B{pp}"])
            p.op("v", lambda v, u=u, a=a, w=w, pp=pp: v.tensor_scalar(
                out=c2q[:, u * TPC + a - 4:u * TPC + a - 4 + w],
                in0=PM[pp][:, 0:w], scalar1=b2q[:, u:u + 1],
                scalar2=None, op0=ALU.add),
                reads=[f"B{pp}", "b2q"], writes=["c2q"])

    for u in range(2):
        for si, (a, b) in enumerate(C2SUBS):
            w = b - a
            pp = (u * len(C2SUBS) + si) % 2
            for k in range(2):
                for tap in range(3):
                    p.op("t", lambda t, u=u, k=k, tap=tap, a=a, w=w, pp=pp: t.matmul(
                        PM[pp][:, 0:w],
                        kc2bf[:, ((u * 2 + k) * 3 + tap) * 128:((u * 2 + k) * 3 + tap) * 128 + 128],
                        c1k[:, k * STRIP + a - 2 + tap:k * STRIP + a - 2 + tap + w],
                        start=(k == 0 and tap == 0), stop=False),
                        reads=["kc2bf", "c1k"], writes=[f"B{pp}"])
            p.op("t", lambda t, u=u, a=a, w=w, pp=pp: t.matmul(
                PM[pp][:, 0:w], ekbf[:, u * 128:u * 128 + 128],
                qmsb[:, a - 4:a - 4 + w], start=False, stop=True),
                reads=["ekbf", "qmsb"], writes=[f"B{pp}"])
            p.op("v", lambda v, u=u, a=a, w=w, pp=pp: v.tensor_scalar(
                out=c2k[:, u * TPC + a - 4:u * TPC + a - 4 + w],
                in0=PM[pp][:, 0:w], scalar1=b2k[:, u:u + 1],
                scalar2=None, op0=ALU.add),
                reads=[f"B{pp}", "b2k"], writes=["c2k"])

    xsq = sqv
    for si, (a, b) in enumerate(C2SUBS):
        w = b - a
        ra = a - 4
        for u in range(8):
            par = u % 2
            p.op("s", lambda s, u=u, ra=ra, w=w, par=par: s.activation(
                out=xsq[par][:, 0:w], in_=c2q[:, u * TPC + ra:u * TPC + ra + w],
                func=AF.Square),
                reads=["c2q"], writes=[f"xsq{par}"])
            p.op("t", lambda t, u=u, w=w, par=par: t.matmul(
                P2[0:16, 0:w], l2qbf[:, u * 16:u * 16 + 16], xsq[par][:, 0:w],
                start=(u == 0), stop=(u == 7)),
                reads=["l2qbf", f"xsq{par}"], writes=["B2"])
        def ev_rq(v, w=w, ra=ra):
            return v.tensor_scalar(out=nsq[0:16, 0:w], in0=P2[0:16, 0:w],
                                   scalar1=1e-24, scalar2=None, op0=ALU.max)
        p.op("v", ev_rq, reads=["B2"], writes=["nsq"])
        p.op("s", lambda s, w=w: s.activation(
            out=nsq[0:16, 0:w], in_=nsq[0:16, 0:w], func=AF.Ln),
            reads=["nsq"], writes=["nsq"])
        p.op("s", lambda s, w=w, ra=ra: s.activation(
            out=rq_f[0:16, ra:ra + w], in_=nsq[0:16, 0:w],
            func=AF.Exp, scale=-0.5),
            reads=["nsq"], writes=["rq_f"])
        p.op("v", lambda v, w=w, ra=ra: v.tensor_copy(
            rqbf[:, ra:ra + w], rq_f[:, ra:ra + w]),
            reads=["rq_f"], writes=["rqbf"])
        for u in range(2):
            par = u % 2
            p.op("s", lambda s, u=u, ra=ra, w=w, par=par: s.activation(
                out=xsq[par][:, 0:w], in_=c2k[:, u * TPC + ra:u * TPC + ra + w],
                func=AF.Square),
                reads=["c2k"], writes=[f"xsq{par}"])
            p.op("t", lambda t, u=u, w=w, par=par: t.matmul(
                P3[0:4, 0:w], l2kbf[:, :], xsq[par][:, 0:w],
                start=(u == 0), stop=(u == 1)),
                reads=["l2kbf", f"xsq{par}"], writes=["B3"])
        def ev_rk(v, w=w):
            return v.tensor_scalar(out=nsq[0:4, 0:w], in0=P3[0:4, 0:w],
                                   scalar1=1e-24, scalar2=None, op0=ALU.max)
        p.op("v", ev_rk, reads=["B3"], writes=["nsq"])
        p.op("s", lambda s, w=w: s.activation(
            out=nsq[0:4, 0:w], in_=nsq[0:4, 0:w], func=AF.Ln),
            reads=["nsq"], writes=["nsq"])
        p.op("s", lambda s, w=w, ra=ra: s.activation(
            out=rk_f[0:4, ra:ra + w], in_=nsq[0:4, 0:w],
            func=AF.Exp, scale=-0.5),
            reads=["nsq"], writes=["rk_f"])
        p.op("v", lambda v, w=w, ra=ra: v.tensor_scalar(
            out=rk_f[0:4, ra:ra + w], in0=rk_f[0:4, ra:ra + w],
            scalar1=kt_sb[0:4, :], scalar2=None, op0=ALU.mult),
            reads=["rk_f", "kt_sb"], writes=["rk_f"])
        p.op("v", lambda v, w=w, ra=ra: v.tensor_copy(
            rkbf[:, ra:ra + w], rk_f[:, ra:ra + w]),
            reads=["rk_f"], writes=["rkbf"])

    def rope(npair, erbf, erkey, ernrows, rbf_t, rkey, src, srckey, stride_u,
             blob, blobname):
        for u in range(npair):
            for si, (a, b) in enumerate(C2SUBS):
                w = b - a
                ra = a - 4
                p.op("t", lambda t, u=u, w=w, ra=ra: t.matmul(
                    P4[:, 0:w], erbf[0:ernrows, u * 128:u * 128 + 128] if npair > 1
                    else erbf[0:ernrows, 0:128],
                    rbf_t[:, ra:ra + w], start=True, stop=True),
                    reads=[erkey, rkey], writes=["B4"])
                up_lo = (u * stride_u + ra, (npair + u) * stride_u + ra)
                p.op("v", lambda v, w=w, ra=ra: v.tensor_tensor(
                    out=ce[:, 0:w], in0=P4[:, 0:w], in1=cosT[:, ra:ra + w],
                    op=ALU.mult), reads=["B4", "cosT"], writes=["ce"])
                p.op("v", lambda v, w=w, ra=ra: v.tensor_tensor(
                    out=se_[:, 0:w], in0=P4[:, 0:w], in1=sinT[:, ra:ra + w],
                    op=ALU.mult), reads=["B4", "sinT"], writes=["se"])
                p.op("v", lambda v, w=w, a0=up_lo[0]: v.tensor_tensor(
                    out=m1[:, 0:w], in0=src[:, a0:a0 + w], in1=ce[:, 0:w],
                    op=ALU.mult), reads=["ce", srckey], writes=["m1"])
                p.op("v", lambda v, w=w, a1=up_lo[1]: v.tensor_tensor(
                    out=m2[:, 0:w], in0=src[:, a1:a1 + w], in1=se_[:, 0:w],
                    op=ALU.mult), reads=["se", srckey], writes=["m2"])
                p.op("v", lambda v, w=w: v.tensor_sub(
                    qup[:, 0:w], m1[:, 0:w], m2[:, 0:w]),
                    reads=["m1", "m2"], writes=["qup"])
                p.op("v", lambda v, w=w, a1=up_lo[1]: v.tensor_tensor(
                    out=m1[:, 0:w], in0=src[:, a1:a1 + w], in1=ce[:, 0:w],
                    op=ALU.mult), reads=["ce", srckey], writes=["m1"])
                p.op("v", lambda v, w=w, a0=up_lo[0]: v.tensor_tensor(
                    out=m2[:, 0:w], in0=src[:, a0:a0 + w], in1=se_[:, 0:w],
                    op=ALU.mult), reads=["se", srckey], writes=["m2"])
                p.op("v", lambda v, w=w: v.tensor_add(
                    qlo[:, 0:w], m1[:, 0:w], m2[:, 0:w]),
                    reads=["m1", "m2"], writes=["qlo"])
                p.op("y", lambda y, u=u, w=w, ra=ra: y.dma_start(
                    out=blob[u][:, ra:ra + w], in_=qup[:, 0:w]),
                    reads=["qup"], writes=["OUT"], is_dma=True)
                p.op("y", lambda y, u=u, w=w, ra=ra: y.dma_start(
                    out=blob[npair + u][:, ra:ra + w], in_=qlo[:, 0:w]),
                    reads=["qlo"], writes=["OUT"], is_dma=True)

    rope(4, erqbf, "erqbf", 16, rqbf, "rqbf", c2q, "c2q", TPC, qblob, "q")
    rope(1, erkbf, "erkbf", 4, rkbf, "rkbf", c2k, "c2k", TPC, kblob, "k")

    for half_ch, (chunk, shift, colbase) in enumerate([(10, 0, 0), (11, -1, 128)]):
        par0 = half_ch % 2
        qkv_stream(chunk, par0)
        for t16 in range(16):
            vp = t16 % 2
            p.op("t", lambda t, t16=t16, shift=shift, par0=par0, vp=vp: t.transpose(
                P7[:, :], qst[par0][:, 4 + 128 * t16 + shift:4 + 128 * t16 + shift + 128],
                identbf[:, :]),
                reads=[f"qst{par0}", "identbf"], writes=["B7"])
            p.op("v", lambda v, vp=vp: v.tensor_copy(vstg[vp][:, :], P7[:, :]),
                 reads=["B7"], writes=[f"vstg{vp}"])
            p.op("y", lambda y, t16=t16, colbase=colbase, vp=vp: y.dma_start(
                out=vblob[128 * t16:128 * t16 + 128, colbase:colbase + 128],
                in_=vstg[vp][:, :]),
                reads=[f"vstg{vp}"], writes=["OUT"], is_dma=True)





def _prep_core_l2(l1_results, ins, c):
    s, g = c // 4, c % 4
    f32 = np.float32
    katt = np.zeros((128, S), BF)
    qatt = np.zeros((2, 128, S), BF)
    vgrid = np.ones((128, 65 * 64), BF)
    for blk in range(4):
        rr = l1_results[4 * s + blk]
        cols = slice(TPC * blk, TPC * (blk + 1))
        kb, qb, vb = rr["kblob"], rr["qblob"], rr["vblob"]
        katt[0:32, cols] = kb[0, 32 * g:32 * g + 32]
        katt[32:64, cols] = kb[1, 32 * g:32 * g + 32]
        katt[64:96, cols] = kb[0, 32 * g:32 * g + 32]
        katt[96:128, cols] = kb[1, 32 * g:32 * g + 32]
        for i in range(2):
            for jj in range(2):
                h_local = 2 * i + jj
                qatt[i][64 * jj:64 * jj + 32, cols] = qb[g][32 * h_local:32 * h_local + 32]
                qatt[i][64 * jj + 32:64 * jj + 64, cols] = qb[4 + g][32 * h_local:32 * h_local + 32]
        vloc = vb[:, 64 * g:64 * g + 64]
        for kc_loc in range(16):
            kc = 16 * blk + kc_loc
            vgrid[:, 65 * kc:65 * kc + 64] = vloc[128 * kc_loc:128 * kc_loc + 128]
    return {"katt": katt, "qatt": qatt, "vgrid": vgrid,
            "kt_rep": np.repeat(np.asarray(ins["L0_attn_key_temp"], f32).reshape(1, 1),
                                128, axis=0)}


def build_launch2():
    nc = bass.Bass("TRN2", target_bir_lowering=False)
    es = ExitStack()
    p = Prog(nc)

    katt_i = nc.dram_tensor("katt", [128, S], BF16, kind="ExternalInput").ap()
    qatt_i = nc.dram_tensor("qatt", [2, 128, S], BF16, kind="ExternalInput").ap()
    vgrid_i = nc.dram_tensor("vgrid", [128, 65 * 64], BF16, kind="ExternalInput").ap()
    kt_rep = nc.dram_tensor("kt_rep", [128, 1], F32, kind="ExternalInput").ap()
    oblob = nc.dram_tensor("oblob", [4, 64, S], BF16, kind="ExternalOutput").ap()

    katt = _sb(nc, es, "katt_s", [128, S], BF16)
    qatt = [_sb(nc, es, f"qatt_s{i}", [128, S], BF16) for i in range(2)]
    vgrid = _sb(nc, es, "vgrid_s", [128, 65 * 64], BF16)
    negsh = _sb(nc, es, "negsh", [128, 1], F32)
    onesf = _sb(nc, es, "onesf", [1, 64], F32)
    ptile_ = [_sb(nc, es, f"ptile{i}", [128, 2048], BF16) for i in range(2)]
    rr_sb = _sb(nc, es, "rr_sb", [1, 512], F32)
    rb_sb = _sb(nc, es, "rb_sb", [64, 512], F32)
    obf = _sb(nc, es, "obf", [64, 512], BF16)

    PS2 = [es.enter_context(nc.psum_tensor(f"PS{i}", [128, 1024], F32))
           for i in range(2)]
    PO = [es.enter_context(nc.psum_tensor(f"PO{j}", [65, 256], F32))
          for j in range(4)]

    p.op("g", lambda g: g.dma_start(out=katt[:, :], in_=katt_i),
         writes=["katt"], is_dma=True)
    for i in range(2):
        p.op("g", lambda g, i=i: g.dma_start(out=qatt[i][:, :], in_=qatt_i[i]),
             writes=[f"qatt{i}"], is_dma=True)
    p.op("g", lambda g: g.dma_start(out=vgrid[:, :], in_=vgrid_i),
         writes=["vgrid"], is_dma=True)
    p.op("y", lambda y: y.dma_start(out=negsh[:, :], in_=kt_rep),
         writes=["negsh"], is_dma=True)
    p.op("s", lambda s: s.activation(out=negsh[:, :], in_=negsh[:, :],
                                     func=AF.Abs, scale=-0.125),
         reads=["negsh"], writes=["negsh"])
    p.op("v", lambda v: v.tensor_scalar(out=negsh[:, :], in0=negsh[:, :],
                                        scalar1=-1.0, scalar2=None,
                                        op0=ALU.mult),
         reads=["negsh"], writes=["negsh"])
    p.op("v", lambda v: v.memset(onesf[:, :], 1.0), writes=["onesf"])

    SLOT_HEAD = [0, 2, 1, 3]
    for qt in range(32):
        q0 = 256 * qt
        nkc = 2 * qt + 2
        for kcx in range(nkc + 1):
            if kcx < nkc:
                kc, kpar = kcx, kcx % 2
                for half in range(2):
                    pb = 64 * half
                    for i in range(2):
                        s_ = 2 * half + i
                        p.op("t", lambda t, i=i, pb=pb, s_=s_, kc=kc, q0=q0, kpar=kpar: t.matmul(
                            PS2[kpar][:, 256 * s_:256 * s_ + 256],
                            katt[pb:pb + 64, 128 * kc:128 * kc + 128],
                            qatt[i][pb:pb + 64, q0:q0 + 256],
                            start=True, stop=True, tile_position=(pb, 0)),
                            reads=["katt", f"qatt{i}"],
                            writes=[f"S{kpar}"], tag=f"qk{qt}:{kc}:{s_}")
                p.op("s", lambda s, kc=kc, kpar=kpar: s.activation(
                    out=ptile_[kpar][:, 0:1024],
                    in_=PS2[kpar][:, 0:1024], func=AF.Exp,
                    bias=negsh[:, :], scale=0.125),
                    reads=[f"S{kpar}", "negsh"],
                    writes=[f"pt{kpar}"], tag=f"exp{qt}:{kc}")
                if kc >= 2 * qt:
                    p.op("g", lambda g, kc=kc, q0=q0, kpar=kpar: g.affine_select(
                        out=ptile_[kpar][:, 0:1024],
                        in_=ptile_[kpar][:, 0:1024],
                        compare_op=ALU.is_ge, fill=0.0,
                        base=q0 - 128 * kc, pattern=[[0, 4], [1, 256]],
                        channel_multiplier=-1),
                        reads=[f"pt{kpar}"], writes=[f"pt{kpar}"],
                        tag=f"mask{qt}:{kc}")
            if kcx >= 1:
                kc, kpar = kcx - 1, (kcx - 1) % 2
                for s_ in range(4):
                    p.op("t", lambda t, s_=s_, kc=kc, nkc=nkc, kpar=kpar: t.matmul(
                        PO[s_][:, :], vgrid[:, 65 * kc:65 * kc + 65],
                        ptile_[kpar][:, 256 * s_:256 * s_ + 256],
                        start=(kc == 0), stop=(kc == nkc - 1)),
                        reads=["vgrid", f"pt{kpar}"],
                        writes=[f"O{s_}"], tag=f"av{qt}:{kc}:{s_}")
        for s_ in range(4):
            hl = SLOT_HEAD[s_]
            p.op("v", lambda v, s_=s_: v.reciprocal(rr_sb[:, 0:256],
                                                    PO[s_][64:65, :]),
                 reads=[f"O{s_}"], writes=["rr"])
            p.op("t", lambda t: t.matmul(
                PS2[0][0:64, 0:256], onesf[:, :], rr_sb[:, 0:256],
                start=True, stop=True),
                reads=["rr", "onesf"], writes=["S0"])
            p.op("v", lambda v: v.tensor_copy(rb_sb[:, 0:256],
                                              PS2[0][0:64, 0:256]),
                 reads=["S0"], writes=["rb"])
            p.op("v", lambda v, s_=s_: v.tensor_tensor(
                out=obf[:, 0:256], in0=PO[s_][0:64, :], in1=rb_sb[:, 0:256],
                op=ALU.mult),
                reads=[f"O{s_}", "rb"], writes=["obf"])
            p.op("y", lambda y, hl=hl, q0=q0: y.dma_start(
                out=oblob[hl][:, q0:q0 + 256], in_=obf[:, 0:256]),
                reads=["obf"], writes=["OUT"], is_dma=True)
    p.op("s", lambda s: None, reads=["OUT"], tag="fence")
    p.emit(es)
    return nc, es



def _prep_core_l3(l2_results, ins, sh, c):
    s, blk = c // 4, c % 4
    t0 = TPC * blk
    olocal = np.zeros((8, 128, TPC), BF)
    for i in range(8):
        g, pair = i // 2, i % 2
        ob = l2_results[4 * s + g]["oblob"]
        olocal[i, 0:64] = ob[2 * pair][:, t0:t0 + TPC]
        olocal[i, 64:128] = ob[2 * pair + 1][:, t0:t0 + TPC]
    d = {"olocal": olocal}
    for k in ("wo_t", "wup_t", "ag36", "lorab_t", "eb4", "ones_row",
              "exp4to32", "ident"):
        d[k] = sh[k]
    return d


H3 = 1024
SUBS3 = [(0, 512), (512, 1024)]


def build_launch3():
    nc = bass.Bass("TRN2", target_bir_lowering=False)
    es = ExitStack()
    p = Prog(nc)

    def din(name, shape, dt=F32):
        return nc.dram_tensor(name, shape, dt, kind="ExternalInput").ap()

    olocal = nc.dram_tensor("olocal", [8, 128, TPC], BF16, kind="ExternalInput").ap()
    x1_in = din("x1_in", [32, 128, TPC])
    wo_t = din("wo_t", [32, 8, 128, 128])
    wup_t = din("wup_t", [32, 32, 128, 128])
    ag36 = din("ag36", [32, 128, 36])
    lorab_t = din("lorab_t", [32, 32, 128])
    eb4 = din("eb4", [4, 1])
    ones_row = din("ones_row", [1, 128])
    exp4to32 = din("exp4to32", [4, 32])
    ident = din("ident", [128, 128])
    out_fm = nc.dram_tensor("out_fm", [32, 128, TPC], F32, kind="ExternalOutput").ap()
    x2_sp = nc.dram_tensor("x2_sp", [32, 128, TPC], F32, kind="Internal").ap()

    eb4sb = _sb(nc, es, "eb4sb", [4, 1], F32)
    onesrbf = _sb(nc, es, "onesrbf", [1, 128], BF16)
    identbf = _sb(nc, es, "identbf", [128, 128], BF16)
    e432bf = _sb(nc, es, "e432bf", [4, 32], BF16)
    onescbf_v = _sb(nc, es, "onescbf", [128, 1], BF16)

    P0 = _ps(nc, es, "P0", [128, 512])
    P1 = _ps(nc, es, "P1", [128, 512])
    P2 = _ps(nc, es, "P2", [128, 512])
    P3 = _ps(nc, es, "P3", [128, 512])
    P4 = _ps(nc, es, "P4", [128, 512])
    P5 = _ps(nc, es, "P5", [128, 512])
    P7 = _ps(nc, es, "P7", [128, 128], BF16)
    PM = [P0, P1]

    ar = Arena(nc, es, 206000)
    AR1 = ar.take(128, 32 * H3, BF16)
    AR2 = ar.take(128, 32 * H3, BF16)
    ag36bf = ar.take(128, 32 * 36, BF16)
    lorabbf = ar.take(32, 32 * 128, BF16)
    olh = AR2[:, 0:8 * H3]
    wt = [ar.take(128, 4096, BF16) for _ in range(2)]
    xsq = [ar.take(128, H3, BF16) for _ in range(2)]
    tmpc = xsq[0]
    mu128 = ar.take(128, H3, BF16)
    r128 = ar.take(128, H3, BF16)
    rowA = ar.take(1, H3, F32)
    rowB = ar.take(1, H3, F32)
    mubf = ar.take(1, H3, BF16)
    rbf = ar.take(1, H3, BF16)
    zsb = ar.take(32, H3, F32)
    zm = ar.take(32, H3, BF16)
    lsb = ar.take(4, H3, F32)
    rowC = lsb[0:1, 0:H3]
    lsbbf = ar.take(4, H3, BF16)
    tmsb = ar.take(128, 32, F32)
    mxsb = ar.take(128, 8, F32)
    gesb = ar.take(128, 32, F32)
    ohsb = ar.take(128, 32, F32)
    notsb = ar.take(128, 8, F32)
    ohbf = ar.take(128, 32, BF16)
    ohfm = ar.take(4, H3, BF16)
    x2f = [ar.take(128, 512, F32) for _ in range(2)]
    xres = [ar.take(128, 512, F32) for _ in range(2)]

    def A1(k):
        return AR1[:, H3 * k:H3 * (k + 1)]

    def A2(k):
        return AR2[:, H3 * k:H3 * (k + 1)]

    p.op("g", lambda g: g.dma_start(out=ag36bf[:, :],
         in_=_rear(ag36, [(36, 128), (4608, 32), (1, 36)])),
         writes=["ag36bf"], is_dma=True)
    p.op("g", lambda g: g.dma_start(out=lorabbf[:, :],
         in_=_rear(lorab_t, [(128, 32), (4096, 32), (1, 128)])),
         writes=["lorabbf"], is_dma=True)
    p.op("g", lambda g: g.dma_start(out=onesrbf[:, :], in_=ones_row),
         writes=["onesrbf"], is_dma=True)
    p.op("g", lambda g: g.dma_start(out=identbf[:, :], in_=ident),
         writes=["identbf"], is_dma=True)
    p.op("g", lambda g: g.dma_start(out=e432bf[:, :], in_=exp4to32),
         writes=["e432bf"], is_dma=True)
    p.op("y", lambda y: y.dma_start(out=eb4sb[:, :], in_=eb4),
         writes=["eb4sb"], is_dma=True)
    p.op("v", lambda v: v.memset(onescbf_v[:, :], 1.0), writes=["onescbf"])

    def stats3(src_of_k, key_of_k):
        PSX = [P2, P3]
        PSQ = [P4, P5]
        for k in range(32):
            par = k % 2
            p.op("s", lambda s, k=k, par=par: s.activation(
                out=xsq[par][:, :], in_=src_of_k(k), func=AF.Square),
                reads=[key_of_k(k)], writes=[f"xsq{par}"])
            for si, (a, b) in enumerate(SUBS3):
                w = b - a
                p.op("t", lambda t, k=k, a=a, w=w, si=si: t.matmul(
                    PSX[si][0:1, 0:w], onescbf_v[:, :], src_of_k(k)[:, a:a + w],
                    start=(k == 0), stop=(k == 31)),
                    reads=[key_of_k(k), "onescbf"], writes=[("B2", "B3")[si]])
                p.op("t", lambda t, k=k, a=a, w=w, par=par, si=si: t.matmul(
                    PSQ[si][0:1, 0:w], onescbf_v[:, :], xsq[par][:, a:a + w],
                    start=(k == 0), stop=(k == 31)),
                    reads=[f"xsq{par}", "onescbf"], writes=[("B4", "B5")[si]])
        def ev_rows(v):
            v.tensor_copy(rowA[0:1, 0:512], PSX[0][0:1, 0:512])
            v.tensor_copy(rowA[0:1, 512:1024], PSX[1][0:1, 0:512])
            v.tensor_copy(rowB[0:1, 0:512], PSQ[0][0:1, 0:512])
            return v.tensor_copy(rowB[0:1, 512:1024], PSQ[1][0:1, 0:512])
        p.op("v", ev_rows, reads=["B2", "B3", "B4", "B5"],
             writes=["rowA", "rowB"])
        p.op("v", lambda v: v.tensor_scalar(
            out=rowA[:, :], in0=rowA[:, :], scalar1=1.0 / 4096,
            scalar2=None, op0=ALU.mult), reads=["rowA"], writes=["rowA"])
        p.op("v", lambda v: v.tensor_scalar(
            out=rowB[:, :], in0=rowB[:, :], scalar1=1.0 / 4096,
            scalar2=None, op0=ALU.mult), reads=["rowB"], writes=["rowB"])
        p.op("v", lambda v: v.tensor_tensor(
            out=rowC[:, :], in0=rowA[:, :], in1=rowA[:, :], op=ALU.mult),
            reads=["rowA"], writes=["lsb"])
        p.op("v", lambda v: v.tensor_tensor(
            out=rowB[:, :], in0=rowB[:, :], in1=rowC[:, :], op=ALU.subtract),
            reads=["rowB", "lsb"], writes=["rowB"])
        p.op("v", lambda v: v.tensor_scalar(
            out=rowB[:, :], in0=rowB[:, :], scalar1=EPS, scalar2=None,
            op0=ALU.add), reads=["rowB"], writes=["rowB"])
        p.op("s", lambda s: s.activation(out=rowC[:, :], in_=rowB[:, :],
                                         func=AF.Ln),
             reads=["rowB"], writes=["lsb"])
        p.op("s", lambda s: s.activation(out=rowC[:, :], in_=rowC[:, :],
                                         func=AF.Exp, scale=-0.5),
             reads=["lsb"], writes=["lsb"])
        p.op("v", lambda v: v.tensor_copy(mubf[:, :], rowA[:, :]),
             reads=["rowA"], writes=["mubf"])
        p.op("v", lambda v: v.tensor_copy(rbf[:, :], rowC[:, :]),
             reads=["lsb"], writes=["rbf"])
        for si, (a, b) in enumerate(SUBS3):
            w = b - a
            p.op("t", lambda t, a=a, w=w: t.matmul(
                P4[:, 0:w], onesrbf[0:1, :], mubf[0:1, a:a + w],
                start=True, stop=True),
                reads=["mubf", "onesrbf"], writes=["B4"])
            p.op("t", lambda t, a=a, w=w: t.matmul(
                P5[:, 0:w], onesrbf[0:1, :], rbf[0:1, a:a + w],
                start=True, stop=True),
                reads=["rbf", "onesrbf"], writes=["B5"])
            p.op("v", lambda v, a=a, w=w: v.tensor_copy(
                mu128[:, a:a + w], P4[:, 0:w]), reads=["B4"], writes=["mu128"])
            p.op("v", lambda v, a=a, w=w: v.tensor_copy(
                r128[:, a:a + w], P5[:, 0:w]), reads=["B5"], writes=["r128"])

    def gate3():
        for k in range(32):
            for si, (a, b) in enumerate(SUBS3):
                dst = [P4, P5][si]
                dkey = ["B4", "B5"][si]
                w = b - a
                p.op("t", lambda t, k=k, a=a, w=w, dst=dst: t.matmul(
                    dst[0:36, 0:w], ag36bf[:, 36 * k:36 * (k + 1)],
                    A2(k)[:, a:a + w], start=(k == 0), stop=(k == 31)),
                    reads=[f"AR2:{k}", "ag36bf"], writes=[dkey])
        def ev_z(v):
            v.tensor_copy(zsb[:, 0:512], P4[0:32, 0:512])
            v.tensor_copy(zsb[:, 512:1024], P5[0:32, 0:512])
            v.tensor_copy(lsb[:, 0:512], P4[32:36, 0:512])
            return v.tensor_copy(lsb[:, 512:1024], P5[32:36, 0:512])
        p.op("v", ev_z, reads=["B4", "B5"], writes=["zsb", "lsb"])
        p.op("v", lambda v: v.tensor_scalar(
            out=lsb[:, :], in0=lsb[:, :], scalar1=eb4sb[:, :],
            scalar2=None, op0=ALU.add), reads=["lsb", "eb4sb"], writes=["lsb"])
        p.op("v", lambda v: v.tensor_copy(lsbbf[:, :], lsb[:, :]),
             reads=["lsb"], writes=["lsbbf"])
        for t8 in range(8):
            p.op("t", lambda t, t8=t8: t.transpose(
                P7[:, 4 * t8:4 * t8 + 4], lsbbf[0:4, 128 * t8:128 * t8 + 128],
                identbf[0:4, 0:4]),
                reads=["lsbbf", "identbf"], writes=["B7"])
        def ev_tm(v):
            tm3 = P7[:, 0:32].rearrange("p (t f) -> p t f", f=4)
            for e in range(4):
                v.tensor_copy(tmsb[:, 8 * e:8 * e + 8], tm3[:, :, e])
            return v.tensor_copy(mxsb[:, :], tm3[:, :, 0])
        p.op("v", ev_tm, reads=["B7"], writes=["tm", "mx"])
        p.op("v", lambda v: v.tensor_tensor(
            out=notsb[:, :], in0=tmsb[:, 0:8], in1=tmsb[:, 8:16], op=ALU.max),
            reads=["tm"], writes=["not"])
        p.op("v", lambda v: v.tensor_tensor(
            out=mxsb[:, :], in0=tmsb[:, 16:24], in1=tmsb[:, 24:32], op=ALU.max),
            reads=["tm"], writes=["mx"])
        p.op("v", lambda v: v.tensor_tensor(
            out=mxsb[:, :], in0=mxsb[:, :], in1=notsb[:, :], op=ALU.max),
            reads=["mx", "not"], writes=["mx"])
        for e in range(4):
            p.op("v", lambda v, e=e: v.tensor_tensor(
                out=gesb[:, 8 * e:8 * e + 8], in0=tmsb[:, 8 * e:8 * e + 8],
                in1=mxsb[:, :], op=ALU.is_ge),
                reads=["tm", "mx"], writes=["ge"])
        p.op("v", lambda v: v.tensor_copy(ohsb[:, 0:8], gesb[:, 0:8]),
             reads=["ge"], writes=["oh"])
        p.op("v", lambda v: v.tensor_scalar(
            out=notsb[:, :], in0=gesb[:, 0:8], scalar1=-1.0, scalar2=1.0,
            op0=ALU.mult, op1=ALU.add), reads=["ge"], writes=["not"])
        for e in range(1, 4):
            p.op("v", lambda v, e=e: v.tensor_tensor(
                out=ohsb[:, 8 * e:8 * e + 8], in0=gesb[:, 8 * e:8 * e + 8],
                in1=notsb[:, :], op=ALU.mult),
                reads=["ge", "not"], writes=["oh"])
            if e < 3:
                p.op("v", lambda v, e=e: v.tensor_scalar(
                    out=mxsb[:, :], in0=gesb[:, 8 * e:8 * e + 8],
                    scalar1=-1.0, scalar2=1.0, op0=ALU.mult, op1=ALU.add),
                    reads=["ge"], writes=["mx"])
                p.op("v", lambda v: v.tensor_tensor(
                    out=notsb[:, :], in0=notsb[:, :], in1=mxsb[:, :],
                    op=ALU.mult), reads=["not", "mx"], writes=["not"])
        p.op("v", lambda v: v.tensor_copy(ohbf[:, :], ohsb[:, :]),
             reads=["oh"], writes=["ohbf"])
        for t8 in range(8):
            par = t8 % 2
            rowb = 0 if par == 0 else 64
            p.op("t", lambda t, t8=t8, rowb=rowb: t.transpose(
                P7[rowb:rowb + 4, 0:128],
                ohbf[0:128, :].rearrange("p (e t) -> p t e", t=8)[:, t8, :],
                identbf[:, :]),
                reads=["ohbf", "identbf"], writes=["B7", f"B7x{par}"])
            p.op("v", lambda v, t8=t8, rowb=rowb: v.tensor_copy(
                ohfm[0:4, 128 * t8:128 * t8 + 128], P7[rowb:rowb + 4, 0:128]),
                reads=[f"B7x{par}"], writes=["ohfm"])
        for si, (a, b) in enumerate(SUBS3):
            dst = [P4, P5][si]
            dkey = ["B4", "B5"][si]
            w = b - a
            p.op("t", lambda t, a=a, w=w, dst=dst: t.matmul(
                dst[0:32, 0:w], e432bf[:, :], ohfm[0:4, a:a + w],
                start=True, stop=True),
                reads=["ohfm", "e432bf"], writes=[dkey])
            p.op("v", lambda v, a=a, w=w, dst=dst, dkey=dkey: v.tensor_tensor(
                out=zm[:, a:a + w], in0=zsb[:, a:a + w], in1=dst[0:32, 0:w],
                op=ALU.mult),
                reads=[dkey, "zsb"], writes=["zm"])

    for h in range(2):
        base = H3 * h
        for k in range(8):
            p.op("g", lambda g, k=k, base=base: g.dma_start(
                out=olh[:, H3 * k:H3 * (k + 1)],
                in_=olocal[k][:, base:base + H3]),
                writes=[f"AR2:{k}"], is_dma=True)
        for o in range(32):
            wpar = o % 2
            p.op("g", lambda g, o=o, wpar=wpar: g.dma_start(
                out=wt[wpar][:, 0:1024],
                in_=_rear(wo_t, [(128, 128), (16384, 8), (1, 128)],
                          offset=o * 8 * 128 * 128)),
                writes=[f"wt{wpar}"], is_dma=True)
            for si, (a, b) in enumerate(SUBS3):
                gi = 2 * o + si
                par = gi % 2
                w = b - a
                for k in range(8):
                    p.op("t", lambda t, k=k, a=a, w=w, par=par, wpar=wpar: t.matmul(
                        PM[par][:, 0:w], wt[wpar][:, 128 * k:128 * (k + 1)],
                        olh[:, H3 * k + a:H3 * k + a + w],
                        start=(k == 0), stop=(k == 7)),
                        reads=[f"wt{wpar}", f"AR2:{k}"], writes=[f"B{par}"])
                p.op("y", lambda y, o=o, a=a, w=w, par=par, base=base: y.dma_start(
                    out=xres[par][:, 0:w], in_=x1_in[o][:, base + a:base + a + w]),
                    writes=[f"xres{par}"], is_dma=True)
                p.op("v", lambda v, a=a, w=w, par=par: v.tensor_tensor(
                    out=x2f[par][:, 0:w], in0=PM[par][:, 0:w],
                    in1=xres[par][:, 0:w], op=ALU.add),
                    reads=[f"B{par}", f"xres{par}"], writes=[f"x2f{par}"])
                p.op("y", lambda y, o=o, a=a, w=w, par=par, base=base: y.dma_start(
                    out=x2_sp[o][:, base + a:base + a + w], in_=x2f[par][:, 0:w]),
                    reads=[f"x2f{par}"], writes=[f"x2sp:{o}:{h}"], is_dma=True)
                p.op("v", lambda v, o=o, a=a, w=w, par=par: v.tensor_copy(
                    A1(o)[:, a:a + w], x2f[par][:, 0:w]),
                    reads=[f"x2f{par}"], writes=[f"AR1:{o}"])
        stats3(A1, lambda k: f"AR1:{k}")
        for k in range(32):
            p.op("v", lambda v, k=k: v.tensor_sub(
                tmpc[:, :], A1(k), mu128[:, :]),
                reads=[f"AR1:{k}", "mu128"], writes=["xsq0"])
            p.op("v", lambda v, k=k: v.tensor_mul(
                A2(k), tmpc[:, :], r128[:, :]),
                reads=["xsq0", "r128"], writes=[f"AR2:{k}"])
        gate3()
        for o in range(32):
            wpar = o % 2
            p.op("g", lambda g, o=o, wpar=wpar: g.dma_start(
                out=wt[wpar][:, :],
                in_=_rear(wup_t, [(128, 128), (16384, 32), (1, 128)],
                          offset=o * 32 * 128 * 128)),
                writes=[f"wt{wpar}"], is_dma=True)
            for si, (a, b) in enumerate(SUBS3):
                gi = 2 * o + si
                par = gi % 2
                w = b - a
                for k in range(32):
                    p.op("t", lambda t, k=k, a=a, w=w, par=par, wpar=wpar: t.matmul(
                        PM[par][:, 0:w], wt[wpar][:, 128 * k:128 * (k + 1)],
                        A2(k)[:, a:a + w], start=(k == 0), stop=False),
                        reads=[f"wt{wpar}", f"AR2:{k}"], writes=[f"B{par}"])
                p.op("t", lambda t, a=a, w=w, par=par, o=o: t.matmul(
                    PM[par][:, 0:w], lorabbf[:, 128 * o:128 * (o + 1)],
                    zm[:, a:a + w], start=False, stop=True),
                    reads=["lorabbf", "zm"], writes=[f"B{par}"])
                p.op("y", lambda y, o=o, a=a, w=w, par=par, base=base: y.dma_start(
                    out=xres[par][:, 0:w], in_=x2_sp[o][:, base + a:base + a + w]),
                    reads=[f"x2sp:{o}:{h}"], writes=[f"xres{par}"], is_dma=True)
                p.op("v", lambda v, a=a, w=w, par=par: v.scalar_tensor_tensor(
                    out=x2f[par][:, 0:w], in0=PM[par][:, 0:w], scalar=0.5,
                    in1=xres[par][:, 0:w], op0=ALU.mult, op1=ALU.add),
                    reads=[f"B{par}", f"xres{par}"], writes=[f"x2f{par}"])
                p.op("y", lambda y, o=o, a=a, w=w, par=par, base=base: y.dma_start(
                    out=out_fm[o][:, base + a:base + a + w], in_=x2f[par][:, 0:w]),
                    reads=[f"x2f{par}"], writes=["OUT"], is_dma=True)
    p.op("s", lambda s: None, reads=["OUT"], tag="fence")
    p.emit(es)
    return nc, es



def kernel(**inputs):
    ins = inputs
    sh = _prep_shared(ins)

    in_maps1 = [_prep_core_l1(ins, sh, c) for c in range(NCORE)]
    nc1, es1 = build_launch1()
    r1 = run_bass_kernel_spmd(nc1, in_maps1, core_ids=list(range(NCORE)))
    l1 = r1.results
    es1.close()

    in_maps2 = [_prep_core_l2(l1, ins, c) for c in range(NCORE)]
    nc2, es2 = build_launch2()
    r2 = run_bass_kernel_spmd(nc2, in_maps2, core_ids=list(range(NCORE)))
    l2 = r2.results
    es2.close()

    in_maps3 = []
    for c in range(NCORE):
        d = _prep_core_l3(l2, ins, sh, c)
        d["x1_in"] = l1[c]["x1_out"]
        in_maps3.append(d)
    nc3, es3 = build_launch3()
    r3 = run_bass_kernel_spmd(nc3, in_maps3, core_ids=list(range(NCORE)))
    l3 = r3.results
    es3.close()

    out = np.empty((B, S, D), np.float32)
    for c in range(NCORE):
        s, blk = c // 4, c % 4
        t0 = TPC * blk
        of = np.asarray(l3[c]["out_fm"])
        out[s, t0:t0 + TPC] = of.reshape(D, TPC).T
    return out
